# version 19
# baseline (speedup 1.0000x reference)
import contextlib
from collections import defaultdict

import numpy as np

import concourse.bass as bass
import concourse.bacc as bacc
import concourse.mybir as mybir
from concourse.bass_utils import run_bass_kernel_spmd

F32 = mybir.dt.float32
BF16 = mybir.dt.bfloat16
I16 = mybir.dt.int16
AF = mybir.ActivationFunctionType
ALU = mybir.AluOpType
AX = mybir.AxisListType

ENGINES = ("tensor", "vector", "scalar", "gpsimd", "sync")
BIGC = 65536.0
EPS = 1e-5


class _Op:
    __slots__ = ("engine", "fn", "reads", "writes", "is_dma", "seq",
                 "sem", "inc_val", "post_val", "waits", "clock", "tag")

    def __init__(self, engine, fn, reads, writes, is_dma, tag=""):
        self.engine = engine
        self.fn = fn
        self.reads = reads
        self.writes = writes
        self.is_dma = is_dma
        self.tag = tag
        self.sem = None
        self.inc_val = 0
        self.post_val = 0
        self.waits = {}
        self.clock = {}


class Prog:
    def __init__(self, nc):
        self.nc = nc
        self.ops = []
        self.last_write = {}
        self.readers = defaultdict(list)
        self.eng_clock = {eng: {} for eng in ENGINES}
        self.sem_counts = defaultdict(int)
        self.dma_channels = set()
        self.sem_objs = {}

    def op(self, engine, fn, reads=(), writes=(), tag=""):
        return self._op(engine, fn, reads, writes, False, None, tag)

    def dma(self, engine, fn, reads=(), writes=(), channel=None, tag=""):
        ch = channel or f"dma_{engine}"
        return self._op(engine, fn, reads, writes, True, ch, tag)

    def _op(self, engine, fn, reads, writes, is_dma, channel, tag):
        o = _Op(engine, fn, tuple(reads), tuple(writes), is_dma, tag)
        o.seq = len(self.ops)
        if is_dma:
            o.sem = f"ch_{channel}"
            o.inc_val = 16
            self.dma_channels.add(o.sem)
        else:
            o.sem = f"eng_{engine}"
            o.inc_val = 1

        deps = []
        for k in o.reads:
            w = self.last_write.get(k)
            if w is not None:
                deps.append(w)
        for k in o.writes:
            w = self.last_write.get(k)
            if w is not None:
                deps.append(w)
            deps.extend(self.readers.get(k, ()))

        need = {}
        for d in deps:
            v = self.sem_counts[d.sem] if d.sem in self.dma_channels else d.post_val
            if need.get(d.sem, 0) < v:
                need[d.sem] = v
        if is_dma:
            prev = self.sem_counts[o.sem]
            if prev > 0 and need.get(o.sem, 0) < prev:
                need[o.sem] = prev

        clk = self.eng_clock[engine]
        o.waits = {s: v for s, v in need.items() if clk.get(s, 0) < v}

        self.sem_counts[o.sem] += o.inc_val
        o.post_val = self.sem_counts[o.sem]

        newclk = dict(clk)
        if engine == "tensor" and not is_dma:
            newclk[o.sem] = o.post_val
        for s, v in need.items():
            if newclk.get(s, 0) < v:
                newclk[s] = v
        for d in deps:
            for s, v in d.clock.items():
                if s in self.dma_channels:
                    continue
                if newclk.get(s, 0) < v:
                    newclk[s] = v
        self.eng_clock[engine] = newclk
        o.clock = dict(newclk)
        o.clock[o.sem] = o.post_val

        for k in o.writes:
            self.last_write[k] = o
            self.readers[k] = []
        for k in o.reads:
            self.readers[k].append(o)
        self.ops.append(o)
        return o

    def barrier(self):
        totals = dict(self.sem_counts)
        for eng in ENGINES:
            clk = self.eng_clock[eng]
            waits = {s: v for s, v in totals.items() if v > 0 and clk.get(s, 0) < v}
            o = _Op(eng, lambda e: e.nop(), (), (), False, f"bar_{eng}")
            o.seq = len(self.ops)
            o.sem = f"eng_{eng}"
            o.inc_val = 1
            o.waits = waits
            self.sem_counts[o.sem] += 1
            o.post_val = self.sem_counts[o.sem]
            newclk = dict(clk)
            for s, v in totals.items():
                if newclk.get(s, 0) < v:
                    newclk[s] = v
            self.eng_clock[eng] = newclk
            o.clock = dict(newclk)
            o.clock[o.sem] = o.post_val
            self.ops.append(o)

    def build(self, block, exitstack):
        nc = self.nc
        for name in sorted(self.sem_counts):
            self.sem_objs[name] = exitstack.enter_context(nc.semaphore(name))
        needed = defaultdict(int)
        for o in self.ops:
            for s, v in o.waits.items():
                if needed[s] < v:
                    needed[s] = v
        per_engine = defaultdict(list)
        for o in self.ops:
            per_engine[o.engine].append(o)

        def emit_stream(eng_name, eng):
            for o in per_engine[eng_name]:
                for s, v in sorted(o.waits.items()):
                    eng.wait_ge(self.sem_objs[s], v)
                inst = o.fn(eng)
                if o.is_dma or needed[o.sem] >= o.post_val:
                    if inst is None:
                        raise RuntimeError(f"op {o.tag or o.seq} needs inc but returned None")
                    inst.then_inc(self.sem_objs[o.sem], o.inc_val)

        for eng_name in ENGINES:
            if not per_engine[eng_name]:
                continue
            getattr(block, eng_name)(lambda eng, en=eng_name: emit_stream(en, eng))


def default_cfg():
    return dict(B=4, T=2048, C=1024, E=8, CAP=320, H=4096, QH=4)


def build_nc(cfg):
    B, T, C, E, CAP, H, QH = (cfg["B"], cfg["T"], cfg["C"], cfg["E"],
                              cfg["CAP"], cfg["H"], cfg["QH"])
    D = C // QH
    NS = C * C // D
    SC = NS // 128
    FB = C // 128
    HB = H // 128
    DB = D // 128
    NT = T // 128
    S = 2 * T
    R = B * CAP
    assert R % 128 == 0 and T % 16 == 0 and S % 512 == 0
    ER = E * R
    DUMP = ER
    RT = R // 128
    GRP = min(512, T)
    TPG = GRP // 128

    nc = bacc.Bacc("TRN2", target_bir_lowering=False)
    x_in = nc.dram_tensor("x", [B, T, C], F32, kind="ExternalInput").ap()
    w_fc = nc.dram_tensor("w_fc", [C, C], F32, kind="ExternalInput").ap()
    w_proj = nc.dram_tensor("w_proj", [C, C], F32, kind="ExternalInput").ap()
    ln_g = nc.dram_tensor("ln_g", [C], F32, kind="ExternalInput").ap()
    ln_b = nc.dram_tensor("ln_b", [C], F32, kind="ExternalInput").ap()
    wr_h = nc.dram_tensor("wr", [C, E], F32, kind="ExternalInput").ap()
    w1_h = nc.dram_tensor("w1", [E, C, H], F32, kind="ExternalInput").ap()
    b1_h = nc.dram_tensor("b1", [E, H], F32, kind="ExternalInput").ap()
    w2_h = nc.dram_tensor("w2", [E, H, C], F32, kind="ExternalInput").ap()
    b2_h = nc.dram_tensor("b2", [E, C], F32, kind="ExternalInput").ap()
    out_h = nc.dram_tensor("out", [B, T, C], F32, kind="ExternalOutput").ap()

    N = B * T
    xn_rows = nc.dram_tensor("xn_rows", [N, C], BF16, kind="Internal").ap()
    s_rows = nc.dram_tensor("s_rows", [N, C], BF16, kind="Internal").ap()
    y_store = nc.dram_tensor("y_store", [ER + 1, C], BF16, kind="Internal").ap()
    ef_h = nc.dram_tensor("ef_h", [B, T, 2], F32, kind="Internal").ap()
    idxq_h = nc.dram_tensor("idxq_h", [B, S], I16, kind="Internal").ap()
    glist_h = nc.dram_tensor("glist_h", [ER + 1, 64], F32, kind="Internal").ap()
    gate_h = nc.dram_tensor("gate_h", [ER + 1, 64], F32, kind="Internal").ap()

    p = Prog(nc)
    ctx = contextlib.ExitStack()

    with ctx:
        def mk(stack):
            def sb(name, shape, dt):
                return stack.enter_context(nc.sbuf_tensor(name, shape, dt))

            def ps(name, shape):
                return stack.enter_context(nc.psum_tensor(name, shape, F32))
            return sb, ps

        sbg, psg = mk(ctx)
        identF = sbg("identF", [128, 128], F32)
        onesK1b = sbg("onesK1b", [1, 128], BF16)
        Rg = cfg["B"] * cfg["CAP"]
        glf = sbg("glf", [16, Rg // 16], F32)
        glx = [sbg(f"glx{i}", [128, Rg // 16], I16) for i in range(2)]
        gatec = [sbg(f"gatec{i}", [128, Rg // 128], F32) for i in range(2)]
        ydix = [[sbg(f"ydix{b_}_{k_}", [128, cfg["T"] // 16], I16) for k_ in range(2)]
                for b_ in range(cfg["B"])]
        b1all = sbg("b1all", [128, E * HB], F32)

        ctxA = contextlib.ExitStack()
        with ctxA:
            sba, psa = mk(ctxA)
            ones18 = sba("ones18", [1, 8], F32)
            ones81 = sba("ones81", [8, 1], F32)
            iotaR = sba("iotaR", [128, E], F32)
            iotaRB = sba("iotaRB", [128, E], F32)
            iotaE8 = sba("iotaE8", [8, 1], F32)
            colER = sba("colER", [8, 1], F32)
            valw = sba("valw", [128, B * NT], F32)
            onesS = sba("onesS", [8, S], BF16)
            zer = sba("zer", [128, 64], F32)
            epsc = sba("epsc", [128, 1], F32)
            gcol = sba("gcol", [128, FB], F32)
            bcol = sba("bcol", [128, FB], F32)
            wr_sb = sba("wr_sb", [128, FB, E], F32)
            sWr = sba("sWr", [128, E], F32)
            kT = sba("kT", [128, DB, NS], BF16)
            v_sb = sba("v_sb", [128, SC, D], BF16)

            p.op("gpsimd", lambda e: e.iota(identF[:], [[1, 128]], channel_multiplier=-1,
                                            allow_small_or_imprecise_dtypes=True),
                 writes=["identF"], tag="io_id")
            p.op("gpsimd", lambda e: e.iota(iotaR[:], [[1, E]], channel_multiplier=0,
                                            allow_small_or_imprecise_dtypes=True),
                 writes=["iotaR"], tag="io_r")
            p.op("gpsimd", lambda e: e.iota(iotaE8[:], [[0, 1]], channel_multiplier=1,
                                            allow_small_or_imprecise_dtypes=True),
                 writes=["iotaE8"], tag="io_e")
            for bb in range(B):
                for tt in range(NT):
                    j = bb * NT + tt
                    p.op("gpsimd", lambda e, j=j, bb=bb, tt=tt: e.iota(
                        valw[:, j:j + 1], [[0, 1]], base=bb * T + tt * 128,
                        channel_multiplier=1, allow_small_or_imprecise_dtypes=True),
                        writes=["valw"], tag=f"io_v{j}")

            p.op("vector", lambda e: e.tensor_scalar(identF[:], identF[:], 0.0, None, ALU.is_equal),
                 reads=["identF"], writes=["identF"], tag="mkident")
            p.op("vector", lambda e: e.tensor_scalar(iotaRB[:], iotaR[:], -BIGC, None, ALU.add),
                 reads=["iotaR"], writes=["iotaRB"])
            p.op("vector", lambda e: e.tensor_scalar(colER[:], iotaE8[:], float(R), None, ALU.mult),
                 reads=["iotaE8"], writes=["colER"])
            p.op("vector", lambda e: e.memset(ones18[:], 1.0), writes=["ones18"])
            p.op("vector", lambda e: e.memset(ones81[:], 1.0), writes=["ones81"])
            p.op("vector", lambda e: e.memset(onesK1b[:], 1.0), writes=["onesK1b"])
            p.op("vector", lambda e: e.memset(onesS[:], 1.0), writes=["onesS"])
            p.op("vector", lambda e: e.memset(zer[:], 0.0), writes=["zer"])
            p.op("vector", lambda e: e.memset(epsc[:], EPS), writes=["epsc"])

            nzb = (ER + 1 + 127) // 128
            for i in range(nzb):
                r0, r1 = i * 128, min((i + 1) * 128, ER + 1)
                p.dma("sync", lambda e, r0=r0, r1=r1: e.dma_start(
                    out=glist_h[r0:r1, :], in_=zer[0:r1 - r0, :]),
                    reads=["zer"], writes=["glist"], channel="zg", tag=f"zg{i}")
                p.dma("sync", lambda e, r0=r0, r1=r1: e.dma_start(
                    out=gate_h[r0:r1, :], in_=zer[0:r1 - r0, :]),
                    reads=["zer"], writes=["gate"], channel="zg", tag=f"zp{i}")

            p.dma("sync", lambda e: e.dma_start(
                out=gcol[:], in_=ln_g.rearrange("(f p) -> p f", p=128), allow_slow_non_contiguous=True),
                writes=["gcol"], channel="cst", tag="lg")
            p.dma("sync", lambda e: e.dma_start(
                out=bcol[:], in_=ln_b.rearrange("(f p) -> p f", p=128), allow_slow_non_contiguous=True),
                writes=["bcol"], channel="cst", tag="lb")
            p.dma("sync", lambda e: e.dma_start(
                out=wr_sb[:], in_=wr_h.rearrange("(f p) e -> p f e", p=128), allow_slow_non_contiguous=True),
                writes=["wr_sb"], channel="cst", tag="lwr")
            p.dma("sync", lambda e: e.dma_start(
                out=b1all[:], in_=b1_h.rearrange("e (h p) -> p (e h)", p=128), allow_slow_non_contiguous=True),
                writes=["b1all"], channel="cst", tag="lb1")

            with contextlib.ExitStack() as c0:
                sb0, ps0 = mk(c0)
                stg = sb0("stg", [128, FB, C], F32)
                tbf = sb0("tbf", [128, FB, C], BF16)
                onesPE = sb0("onesPE", [128, 128], F32)
                psT0 = ps0("psT0", [128, 1024])
                psW = ps0("psW", [128, 512])
                p.op("vector", lambda e: e.memset(onesPE[:], 1.0), writes=["onesPE"])
                p.op("vector", lambda e: e.memset(v_sb[:], 0.0), writes=["v_sb"])
                p.op("vector", lambda e: e.memset(kT[:], 0.0), writes=["kT"])

                for wname, hbuf in (("fc", w_fc), ("pj", w_proj)):
                    p.dma("sync", lambda e, hbuf=hbuf: e.dma_start(
                        out=stg[:], in_=hbuf.rearrange("(a p) c -> p a c", p=128)),
                        reads=["stg"], writes=["stg"], channel="cst", tag=f"lw{wname}")
                    for rb in range(FB):
                        for cb in range(FB):
                            par = (rb * FB + cb) % 2
                            pk_ = ("psT0", par)
                            p.op("tensor", lambda e, rb=rb, cb=cb, par=par: e.transpose(
                                psT0[:, par * 512:par * 512 + 128],
                                stg[:, rb, cb * 128:(cb + 1) * 128], identF[:]),
                                reads=["stg", "identF"], writes=[pk_], tag=f"T{wname}{rb}{cb}")
                            p.op("vector", lambda e, rb=rb, cb=cb, par=par: e.tensor_copy(
                                tbf[:, cb, rb * 128:(rb + 1) * 128],
                                psT0[:, par * 512:par * 512 + 128]),
                                reads=[pk_], writes=["tbf"], tag=f"E{wname}{rb}{cb}")
                    if wname == "fc":
                        for db in range(DB):
                            for sr in range(4):
                                cb = (sr * D + db * 128) // 128
                                p.op("vector", lambda e, db=db, sr=sr, cb=cb: e.tensor_copy(
                                    kT[:, db, sr::4], tbf[:, cb, 0:NS // 4]),
                                    reads=["tbf"], writes=["kT"], tag=f"k{db}{sr}")
                        for fb in range(FB):
                            p.op("tensor", lambda e, fb=fb: e.matmul(
                                psW[:, 0:E], onesPE[:], wr_sb[:, fb, :],
                                start=(fb == 0), stop=(fb == FB - 1)),
                                reads=["onesPE", "wr_sb"], writes=["psW"], tag=f"sw{fb}")
                        p.op("vector", lambda e: e.tensor_copy(sWr[:], psW[:, 0:E]),
                             reads=["psW"], writes=["sWr"])
                    else:
                        for q in range(4):
                            for w in range(4):
                                p.dma("scalar", lambda e, q=q, w=w: e.dma_start(
                                    out=v_sb[q:q + 125:4, :, :].rearrange(
                                        "p (u w2) d -> p u w2 d", w2=4)[:, :, w, :],
                                    in_=tbf[32 * w:32 * w + 32, :, q * D:(q + 1) * D]),
                                    reads=["tbf"], writes=["v_sb"], channel="vsh",
                                    tag=f"v{q}{w}")
                p.barrier()
            sWr_key = "sWr"

            xt = [sba(f"xt{i}", [128, C], F32) for i in range(3)]
            st6_ = [sba(f"st6{i}", [128, 12], F32) for i in range(2)]
            mv_ = [sba(f"mv{i}", [128, 2], F32) for i in range(2)]
            sc1_ = [sba(f"sc1{i}", [128, 4], F32) for i in range(2)]
            sd2_ = [sba(f"sd2{i}", [128, 1], F32) for i in range(2)]
            mu2_ = [sba(f"mu2{i}", [128, 1], F32) for i in range(2)]
            r2c_ = [sba(f"r2c{i}", [128, 2], F32) for i in range(2)]
            xn_f = [sba(f"xn_f{i}", [128, C], F32) for i in range(2)]
            xnb = [sba(f"xnb{i}", [128, C], BF16) for i in range(2 * TPG)]
            ap_f = [sba(f"ap_f{i}", [128, C], F32) for i in range(2)]
            apT = [sba(f"apT{i}", [128, C], F32) for i in range(2)]
            aiT_ = [sba(f"aiT{i}", [128, FB, GRP], BF16) for i in range(2)]
            pT = [sba(f"pT{i}", [128, GRP], BF16) for i in range(2)]
            uT = [sba(f"uT{i}", [128, GRP], BF16) for i in range(2)]
            sev = [sba(f"sev{i}", [128, C], BF16) for i in range(TPG)]
            lgt_ = [sba(f"lgt{i}", [128, E], F32) for i in range(2)]
            eqt_ = [sba(f"eqt{i}", [128, E], F32) for i in range(2)]
            cnd_ = [sba(f"cnd{i}", [128, E], F32) for i in range(2)]
            msk_ = [sba(f"msk{i}", [128, E], F32) for i in range(2)]
            exq_ = [sba(f"exq{i}", [128, E], F32) for i in range(2)]
            mm1_ = [sba(f"mm1{i}", [128, 1], F32) for i in range(2)]
            mm2t_ = [sba(f"mm2t{i}", [128, 1], F32) for i in range(2)]
            dent_ = [sba(f"dent{i}", [128, 1], F32) for i in range(2)]
            ipack_ = [sba(f"ipack{i}", [128, 2], F32) for i in range(2)]
            pk0sb = [sba(f"pk0sb{i}", [128, NT], F32) for i in range(B)]
            pk1sb = [sba(f"pk1sb{i}", [128, NT], F32) for i in range(B)]
            gA = sba("gA", [8, S], F32)
            gB = sba("gB", [8, S], F32)
            gC = sba("gC", [8, S], F32)
            efr = sba("efr", [1, S], F32)
            idxi = sba("idxi", [1, S], I16)
            idxw_ = [sba(f"idxw{i}", [128, T // 16], I16) for i in range(2)]

            psT = [psa("psTs0", [128, 512])]
            psLG = psa("psLG", [128, 512])
            psSC = [psa(f"psSC{i}", [128, 512]) for i in range(2)]
            psO = psa("psO", [128, TPG * 512])

            ORDER = [(bb, tt) for bb in range(B) for tt in range(NT)]

            def xload(step):
                if step >= len(ORDER):
                    return
                bb, tt = ORDER[step]
                p.dma("sync", lambda e, bb=bb, tt=tt, step=step: e.dma_start(
                    out=xt[step % 3][:], in_=x_in[bb, tt * 128:(tt + 1) * 128, :]),
                    reads=[f"xt{step % 3}"], writes=[f"xt{step % 3}"],
                    channel=f"xl{step % 3}", tag=f"xl{step}")

            xload(0)
            xload(1)
            xload(2)

            def emit_tile_p1(bb, tt, estep):
                    idx = bb * NT + tt
                    gpar = (idx // TPG) % 2
                    tp_ = estep % 2
                    st6 = st6_[tp_]
                    mv = mv_[tp_]
                    sc1 = sc1_[tp_]
                    sd2 = sd2_[tp_]
                    mu2 = mu2_[tp_]
                    r2c = r2c_[tp_]
                    lgt = lgt_[tp_]
                    eqt = eqt_[tp_]
                    cnd = cnd_[tp_]
                    msk = msk_[tp_]
                    exq = exq_[tp_]
                    mm1 = mm1_[tp_]
                    mm2t = mm2t_[tp_]
                    dent = dent_[tp_]
                    ipack = ipack_[tp_]
                    bpar = bb
                    aiT = aiT_[gpar]
                    X = xt[estep % 3]
                    xk = f"xt{estep % 3}"
                    XN = xn_f[estep % 2]
                    xnk = f"xn_f{estep % 2}"
                    XNB = xnb[gpar * TPG + tt % TPG]
                    xnbk = f"xnb{gpar * TPG + tt % TPG}"
                    AP_ = ap_f[estep % 2]
                    apk = f"ap_f{estep % 2}"
                    APT = apT[estep % 2]
                    aptk = f"apT{estep % 2}"

                    p.op("vector", lambda e, X=X: e.bn_stats(st6[:, 0:6], X[:, 0:C // 2]),
                         reads=[xk], writes=[f"st6{tp_}"], tag=f"bs1a{idx}")
                    p.op("vector", lambda e, X=X: e.bn_stats(st6[:, 6:12], X[:, C // 2:C]),
                         reads=[xk], writes=[f"st6{tp_}"], tag=f"bs1b{idx}")
                    p.op("vector", lambda e: e.bn_aggr(mv[:], st6[:]),
                         reads=[f"st6{tp_}"], writes=[f"mv{tp_}"], tag=f"ba1{idx}")
                    p.op("scalar", lambda e: e.activation(sc1[:, 0:1], mv[:, 1:2], AF.Sqrt, bias=epsc[:]),
                         reads=[f"mv{tp_}", "epsc"], writes=[f"sc1{tp_}"], tag=f"sq1{idx}")
                    p.op("vector", lambda e: e.reciprocal(sc1[:, 1:2], sc1[:, 0:1]),
                         reads=[f"sc1{tp_}"], writes=[f"sc1{tp_}"], tag=f"rc1{idx}")
                    p.op("vector", lambda e: e.scalar_tensor_tensor(
                        sc1[:, 2:3], mv[:, 0:1], -1.0, sc1[:, 1:2], ALU.mult, ALU.mult),
                        reads=[f"mv{tp_}", f"sc1{tp_}"], writes=[f"sc1{tp_}"], tag=f"b1c{idx}")
                    p.op("scalar", lambda e, X=X, XN=XN: e.activation(
                        XN[:], X[:], AF.Identity, bias=sc1[:, 2:3], scale=sc1[:, 1:2]),
                        reads=[xk, f"sc1{tp_}"], writes=[xnk], tag=f"xn{idx}")
                    p.op("gpsimd", lambda e, XN=XN, XNB=XNB: e.tensor_copy(
                        XNB[:], XN[:]),
                        reads=[xnk], writes=[xnbk], tag=f"xnbc{idx}")
                    p.dma("scalar", lambda e, XNB=XNB, bb=bb, tt=tt: e.dma_start(
                        out=xn_rows[bb * T + tt * 128: bb * T + (tt + 1) * 128, :], in_=XNB[:]),
                        reads=[xnbk], writes=["xn_rows"], channel=f"xs{estep % 2}", tag=f"xns{idx}")
                    p.op("vector", lambda e, XN=XN: e.bn_stats(st6[:, 0:6], XN[:, 0:C // 2]),
                         reads=[xnk], writes=[f"st6{tp_}"], tag=f"bs2a{idx}")
                    p.op("vector", lambda e, XN=XN: e.bn_stats(st6[:, 6:12], XN[:, C // 2:C]),
                         reads=[xnk], writes=[f"st6{tp_}"], tag=f"bs2b{idx}")
                    p.op("vector", lambda e: e.bn_aggr(mv[:], st6[:]),
                         reads=[f"st6{tp_}"], writes=[f"mv{tp_}"], tag=f"ba2{idx}")
                    p.op("scalar", lambda e: e.activation(sd2[:], mv[:, 1:2], AF.Sqrt, bias=epsc[:]),
                         reads=[f"mv{tp_}", "epsc"], writes=[f"sd2{tp_}"], tag=f"sq2{idx}")
                    p.op("vector", lambda e: e.tensor_copy(mu2[:], mv[:, 0:1]),
                         reads=[f"mv{tp_}"], writes=[f"mu2{tp_}"], tag=f"mu2{idx}")
                    p.op("vector", lambda e: e.reciprocal(r2c[:, 0:1], sd2[:]),
                         reads=[f"sd2{tp_}"], writes=[f"r2c{tp_}"], tag=f"rc2{idx}")
                    p.op("vector", lambda e: e.scalar_tensor_tensor(
                        r2c[:, 1:2], mu2[:], -1.0, r2c[:, 0:1], ALU.mult, ALU.mult),
                        reads=[f"mu2{tp_}", f"r2c{tp_}"], writes=[f"r2c{tp_}"], tag=f"b2c{idx}")
                    p.op("scalar", lambda e, XN=XN, AP_=AP_: e.activation(
                        AP_[:], XN[:], AF.Identity, bias=r2c[:, 1:2], scale=r2c[:, 0:1]),
                        reads=[xnk, f"r2c{tp_}"], writes=[apk], tag=f"apa{idx}")
                    for fb in range(FB):
                        pk_ = ("psT", 0)
                        p.op("tensor", lambda e, AP_=AP_, fb=fb: e.transpose(
                            psT[0][:, 0:128],
                            AP_[:, fb * 128:(fb + 1) * 128], identF[:]),
                            reads=[apk, "identF"], writes=[pk_], tag=f"tr{idx}_{fb}")
                        p.op("vector", lambda e, fb=fb, APT=APT: e.tensor_copy(
                            APT[:, fb * 128:(fb + 1) * 128],
                            psT[0][:, 0:128]),
                            reads=[pk_], writes=[aptk], tag=f"ae{idx}_{fb}")
                        p.op("scalar", lambda e, fb=fb, tt=tt, APT=APT: e.activation(
                            aiT[:, fb, (tt % TPG) * 128:(tt % TPG + 1) * 128],
                            APT[:, fb * 128:(fb + 1) * 128],
                            AF.Identity, bias=bcol[:, fb:fb + 1], scale=gcol[:, fb:fb + 1]),
                            reads=[aptk, "gcol", "bcol"], writes=[f"aiT{gpar}"], tag=f"ai{idx}_{fb}")
                    for fb in range(FB):
                        p.op("tensor", lambda e, fb=fb, APT=APT: e.matmul(
                            psLG[:, 0:E], APT[:, fb * 128:(fb + 1) * 128], wr_sb[:, fb, :],
                            start=(fb == 0), stop=(fb == FB - 1)),
                            reads=[aptk, "wr_sb"], writes=["psLG"], tag=f"rt{idx}_{fb}")
                    p.op("vector", lambda e: e.tensor_scalar(lgt[:], psLG[:, 0:E], sd2[:], None, ALU.mult),
                         reads=["psLG", f"sd2{tp_}"], writes=[f"lgt{tp_}"], tag=f"lg1{idx}")
                    p.op("vector", lambda e: e.scalar_tensor_tensor(
                        lgt[:], sWr[:], mu2[:], lgt[:], ALU.mult, ALU.add),
                        reads=[sWr_key, f"mu2{tp_}", f"lgt{tp_}"], writes=[f"lgt{tp_}"], tag=f"lg2{idx}")

                    xload(estep + 3)

                    if tt % TPG == TPG - 1:
                        g0 = (tt // TPG) * GRP
                        for h in range(QH):
                            for scj in range(SC + 1):
                                if scj < SC:
                                    for db in range(DB):
                                        p.op("tensor", lambda e, h=h, scj=scj, db=db: e.matmul(
                                            psSC[scj % 2][:, 0:GRP],
                                            kT[:, db, scj * 128:(scj + 1) * 128],
                                            aiT[:, h * DB + db, :],
                                            start=(db == 0), stop=(db == DB - 1),
                                            skip_group_check=True),
                                            reads=["kT", f"aiT{gpar}"], writes=[f"psSC{scj % 2}"],
                                            tag=f"qk{idx}_{h}_{scj}_{db}")
                                if scj > 0:
                                    sc_ = scj - 1
                                    p.op("scalar", lambda e, sc_=sc_: e.activation(
                                        uT[sc_ % 2][:], psSC[sc_ % 2][:, 0:GRP], AF.Relu),
                                        reads=[f"psSC{sc_ % 2}"], writes=[f"uT{sc_ % 2}"],
                                        tag=f"ru_{idx}_{h}_{sc_}")
                                    p.op("vector", lambda e, sc_=sc_: e.tensor_tensor(
                                        pT[sc_ % 2][:], uT[sc_ % 2][:],
                                        psSC[sc_ % 2][:, 0:GRP], ALU.mult),
                                        reads=[f"uT{sc_ % 2}", f"psSC{sc_ % 2}"],
                                        writes=[f"pT{sc_ % 2}"],
                                        tag=f"r2_{idx}_{h}_{sc_}")
                                    for st4 in range(TPG):
                                        p.op("tensor", lambda e, sc_=sc_, st4=st4: e.matmul(
                                            psO[:, st4 * 512:st4 * 512 + D],
                                            pT[sc_ % 2][:, st4 * 128:(st4 + 1) * 128],
                                            v_sb[:, sc_, :],
                                            start=(sc_ == 0), stop=(sc_ == SC - 1),
                                            skip_group_check=True),
                                            reads=[f"pT{sc_ % 2}", "v_sb"],
                                            writes=[("psO", st4)],
                                            tag=f"pv{idx}_{h}_{sc_}_{st4}")
                            for st4 in range(TPG):
                                XB = xnb[gpar * TPG + st4]
                                p.op("vector", lambda e, h=h, st4=st4, XB=XB: e.tensor_tensor(
                                    sev[st4][:, h * D:(h + 1) * D],
                                    psO[:, st4 * 512:st4 * 512 + D],
                                    XB[:, h * D:(h + 1) * D], ALU.add),
                                    reads=[("psO", st4), f"xnb{gpar * TPG + st4}"],
                                    writes=[f"sev{st4}"], tag=f"sv{idx}_{h}_{st4}")
                                if h == QH - 1:
                                    p.dma("scalar", lambda e, bb=bb, g0=g0, st4=st4: e.dma_start(
                                        out=s_rows[bb * T + g0 + st4 * 128:
                                                   bb * T + g0 + (st4 + 1) * 128, :],
                                        in_=sev[st4][:]),
                                        reads=[f"sev{st4}"], writes=["s_rows"],
                                        channel=f"ss{st4 % 2}", tag=f"ssv{idx}_{st4}")


            def emit_tile_p2(bb, tt, estep):
                    idx = bb * NT + tt
                    gpar = (idx // TPG) % 2
                    tp_ = estep % 2
                    st6 = st6_[tp_]
                    mv = mv_[tp_]
                    sc1 = sc1_[tp_]
                    sd2 = sd2_[tp_]
                    mu2 = mu2_[tp_]
                    r2c = r2c_[tp_]
                    lgt = lgt_[tp_]
                    eqt = eqt_[tp_]
                    cnd = cnd_[tp_]
                    msk = msk_[tp_]
                    exq = exq_[tp_]
                    mm1 = mm1_[tp_]
                    mm2t = mm2t_[tp_]
                    dent = dent_[tp_]
                    ipack = ipack_[tp_]
                    bpar = bb
                    p.op("vector", lambda e: e.tensor_reduce(mm1[:], lgt[:], AX.X, ALU.max),
                         reads=[f"lgt{tp_}"], writes=[f"mm1{tp_}"], tag=f"m1{idx}")
                    p.op("vector", lambda e: e.tensor_scalar(eqt[:], lgt[:], mm1[:], None, ALU.is_equal),
                         reads=[f"lgt{tp_}", f"mm1{tp_}"], writes=[f"eqt{tp_}"], tag=f"eqa{idx}")
                    p.op("vector", lambda e: e.tensor_tensor(cnd[:], eqt[:], iotaRB[:], ALU.mult),
                         reads=[f"eqt{tp_}", "iotaRB"], writes=[f"cnd{tp_}"], tag=f"c1{idx}")
                    p.op("vector", lambda e: e.tensor_scalar(cnd[:], cnd[:], BIGC, None, ALU.add),
                         reads=[f"cnd{tp_}"], writes=[f"cnd{tp_}"], tag=f"c1b{idx}")
                    p.op("vector", lambda e: e.tensor_reduce(ipack[:, 0:1], cnd[:], AX.X, ALU.min),
                         reads=[f"cnd{tp_}"], writes=[f"ipack{tp_}"], tag=f"i1{idx}")
                    p.op("vector", lambda e: e.tensor_scalar(exq[:], lgt[:], mm1[:], None, ALU.subtract),
                         reads=[f"lgt{tp_}", f"mm1{tp_}"], writes=[f"exq{tp_}"], tag=f"exs{idx}")
                    p.op("scalar", lambda e: e.activation(exq[:], exq[:], AF.Exp),
                         reads=[f"exq{tp_}"], writes=[f"exq{tp_}"], tag=f"exe{idx}")
                    p.op("vector", lambda e: e.tensor_reduce(dent[:], exq[:], AX.X, ALU.add),
                         reads=[f"exq{tp_}"], writes=[f"dent{tp_}"], tag=f"dn{idx}")
                    p.op("vector", lambda e: e.reciprocal(pk0sb[bpar][:, tt:tt + 1], dent[:]),
                         reads=[f"dent{tp_}"], writes=[f"pk0sb{bpar}"], tag=f"p1{idx}")
                    p.op("vector", lambda e: e.tensor_scalar(eqt[:], iotaR[:], ipack[:, 0:1], None, ALU.is_equal),
                         reads=["iotaR", f"ipack{tp_}"], writes=[f"eqt{tp_}"], tag=f"eqb{idx}")
                    p.op("vector", lambda e: e.scalar_tensor_tensor(
                        msk[:], eqt[:], -BIGC, lgt[:], ALU.mult, ALU.add),
                        reads=[f"eqt{tp_}", f"lgt{tp_}"], writes=[f"msk{tp_}"], tag=f"mka{idx}")
                    p.op("vector", lambda e: e.tensor_reduce(mm2t[:], msk[:], AX.X, ALU.max),
                         reads=[f"msk{tp_}"], writes=[f"mm2t{tp_}"], tag=f"m2{idx}")
                    p.op("vector", lambda e: e.tensor_scalar(eqt[:], msk[:], mm2t[:], None, ALU.is_equal),
                         reads=[f"msk{tp_}", f"mm2t{tp_}"], writes=[f"eqt{tp_}"], tag=f"eqc{idx}")
                    p.op("vector", lambda e: e.tensor_tensor(cnd[:], eqt[:], iotaRB[:], ALU.mult),
                         reads=[f"eqt{tp_}", "iotaRB"], writes=[f"cnd{tp_}"], tag=f"c2{idx}")
                    p.op("vector", lambda e: e.tensor_scalar(cnd[:], cnd[:], BIGC, None, ALU.add),
                         reads=[f"cnd{tp_}"], writes=[f"cnd{tp_}"], tag=f"c2b{idx}")
                    p.op("vector", lambda e: e.tensor_reduce(ipack[:, 1:2], cnd[:], AX.X, ALU.min),
                         reads=[f"cnd{tp_}"], writes=[f"ipack{tp_}"], tag=f"i2{idx}")
                    p.op("vector", lambda e: e.tensor_scalar(mm2t[:], mm2t[:], mm1[:], None, ALU.subtract),
                         reads=[f"mm2t{tp_}", f"mm1{tp_}"], writes=[f"mm2t{tp_}"], tag=f"d2{idx}")
                    p.op("scalar", lambda e: e.activation(mm2t[:], mm2t[:], AF.Exp),
                         reads=[f"mm2t{tp_}"], writes=[f"mm2t{tp_}"], tag=f"d2e{idx}")
                    p.op("vector", lambda e: e.tensor_tensor(
                        pk1sb[bpar][:, tt:tt + 1], mm2t[:], pk0sb[bpar][:, tt:tt + 1], ALU.mult),
                        reads=[f"mm2t{tp_}", f"pk0sb{bpar}"], writes=[f"pk1sb{bpar}"], tag=f"p2{idx}")
                    p.dma("scalar", lambda e, bb=bb, tt=tt: e.dma_start(
                        out=ef_h[bb, tt * 128:(tt + 1) * 128, :], in_=ipack[:]),
                        reads=[f"ipack{tp_}"], writes=["ef_h"], channel="pks", tag=f"se{idx}")


            def emit_routing(bb):
                p.dma("gpsimd", lambda e, bb=bb: e.dma_start(
                    out=efr[:], in_=ef_h[bb].rearrange("t k -> (t k)").unsqueeze(0)),
                    reads=["ef_h", "efr"], writes=["efr"], channel="rld", tag=f"lef{bb}")
                for c0 in range(0, S, 512):
                    p.op("tensor", lambda e, c0=c0: e.matmul(
                        psSC[0][0:8, :], ones18[:], efr[:, c0:c0 + 512],
                        start=True, stop=True),
                        reads=["efr", "ones18"], writes=["psSC0"], tag=f"bce{bb}_{c0}")
                    p.op("vector", lambda e, c0=c0: e.tensor_scalar(
                        gA[:, c0:c0 + 512], psSC[0][0:8, :], iotaE8[:], None, ALU.is_equal),
                        reads=["psSC0", "iotaE8"], writes=["gA"], tag=f"oh{bb}_{c0}")
                p.op("vector", lambda e: e.tensor_tensor_scan(
                    gB[:], gA[:], onesS[:], 0.0, ALU.add, ALU.mult),
                    reads=["gA", "onesS"], writes=["gB"], tag=f"scan{bb}")
                p.op("vector", lambda e: e.scalar_tensor_tensor(
                    gC[:], gB[:], float(CAP), gA[:], ALU.is_le, ALU.mult),
                    reads=["gB", "gA"], writes=["gC"], tag=f"keep{bb}")
                p.op("vector", lambda e: e.tensor_scalar(
                    gB[:], gB[:], -1.0, float(CAP - 1), ALU.add, ALU.min),
                    reads=["gB"], writes=["gB"], tag=f"posc{bb}")
                p.op("vector", lambda e, bb=bb: e.tensor_scalar(
                    gB[:], gB[:], colER[:], float(bb * CAP - DUMP), ALU.add, ALU.add),
                    reads=["gB", "colER"], writes=["gB"], tag=f"poff{bb}")
                p.op("vector", lambda e: e.tensor_tensor(gC[:], gC[:], gB[:], ALU.mult),
                     reads=["gC", "gB"], writes=["gC"], tag=f"idxg{bb}")
                for c0 in range(0, S, 512):
                    p.op("tensor", lambda e, c0=c0: e.matmul(
                        psSC[1][0:1, :], ones81[:], gC[:, c0:c0 + 512],
                        start=True, stop=True),
                        reads=["gC", "ones81"], writes=["psSC1"], tag=f"sid{bb}_{c0}")
                    p.op("vector", lambda e, c0=c0: e.tensor_scalar(
                        efr[:, c0:c0 + 512], psSC[1][0:1, :], float(DUMP), None, ALU.add),
                        reads=["psSC1"], writes=["efr"], tag=f"idq{bb}_{c0}")
                p.op("vector", lambda e: e.tensor_copy(idxi[:], efr[:]),
                     reads=["efr"], writes=["idxi"], tag=f"cvt{bb}")
                p.dma("gpsimd", lambda e, bb=bb: e.dma_start(out=idxq_h[bb, :], in_=idxi[0:1, :]),
                      reads=["idxi"], writes=["idxq_h"], channel="rst", tag=f"sxq{bb}")
                for k in range(2):
                    idxw = idxw_[k]
                    p.dma("gpsimd", lambda e, bb=bb, k=k, idxw=idxw: e.dma_start(
                        out=idxw[0:16, :],
                        in_=idxq_h[bb].rearrange("(c p k) -> k p c", p=16, k=2)[k],
                        allow_slow_non_contiguous=True),
                        reads=["idxq_h", f"idxw{k}"], writes=[f"idxw{k}"], channel=f"iw{k}",
                        tag=f"wr{bb}{k}")
                    for rep in range(1, 8):
                        p.dma("gpsimd", lambda e, rep=rep, idxw=idxw: e.dma_start(
                            out=idxw[16 * rep:16 * rep + 16, :], in_=idxw[0:16, :]),
                            reads=[f"idxw{k}"], writes=[f"idxw{k}"], channel=f"iwr{k}",
                            tag=f"wrr{bb}{k}{rep}")
                    p.dma("gpsimd", lambda e, bb=bb, k=k: e.dma_scatter_add(
                        out_ap=glist_h[:, 0:1],
                        in_ap=valw[:, bb * NT:(bb + 1) * NT].rearrange("p (a c) -> p c a", a=1),
                        idxs_ap=idxw_[k][:], num_idxs=T, num_idxs_reg=T, elem_size=1, elem_step=64),
                        reads=["valw", f"idxw{k}", "glist"], writes=["glist"],
                        channel="sctg", tag=f"gsc{bb}{k}")
                    pksrc = (pk0sb, pk1sb)[k][bb]
                    pkkey = f"pk{k}sb{bb}"
                    p.dma("gpsimd", lambda e, pksrc=pksrc, k=k: e.dma_scatter_add(
                        out_ap=gate_h[:, 0:1],
                        in_ap=pksrc[:].rearrange("p (c a) -> p c a", a=1),
                        idxs_ap=idxw_[k][:], num_idxs=T, num_idxs_reg=T, elem_size=1, elem_step=64),
                        reads=[pkkey, f"idxw{k}", "gate"], writes=["gate", pkkey],
                        channel="sctp", tag=f"psc{bb}{k}")
                    p.dma("gpsimd", lambda e, bb=bb, k=k: e.dma_start(
                        out=ydix[bb][k][0:16, :],
                        in_=idxq_h[bb].rearrange("(c p k) -> k p c", p=16, k=2)[k],
                        allow_slow_non_contiguous=True),
                        reads=["idxq_h", f"ydix{bb}_{k}"], writes=[f"ydix{bb}_{k}"],
                        channel=f"iw{k}", tag=f"ywA{bb}{k}")
                    for rep in range(1, 8):
                        p.dma("gpsimd", lambda e, bb=bb, k=k, rep=rep: e.dma_start(
                            out=ydix[bb][k][16 * rep:16 * rep + 16, :], in_=ydix[bb][k][0:16, :]),
                            reads=[f"ydix{bb}_{k}"], writes=[f"ydix{bb}_{k}"],
                            channel=f"iwr{k}", tag=f"ywAr{bb}{k}{rep}")

            GCHa = 640 if R % 640 == 0 else min(R, 512)
            NGCHa = R // GCHa

            def prep_lists(e_):
                pe = e_ % 2
                for gq in range(NGCHa):
                    r0 = gq * GCHa
                    p.dma("sync", lambda e, e_=e_, r0=r0: e.dma_start(
                        out=glf[:, 0:GCHa // 16],
                        in_=glist_h[e_ * R + r0:e_ * R + r0 + GCHa, 0:1].rearrange(
                            "(c p) o -> p (c o)", p=16),
                        allow_slow_non_contiguous=True),
                        reads=["glist", "glf"], writes=["glf"], channel=f"glr{e_ % 2}",
                        tag=f"glf{e_}_{gq}")
                    p.op("vector", lambda e, pe=pe, gq=gq: e.tensor_copy(
                        glx[pe][0:16, gq * (GCHa // 16):(gq + 1) * (GCHa // 16)],
                        glf[:, 0:GCHa // 16]),
                        reads=["glf"], writes=[f"glx{pe}"], tag=f"glc{e_}_{gq}")
                for rep in range(1, 8):
                    p.dma("scalar", lambda e, pe=pe, rep=rep: e.dma_start(
                        out=glx[pe][16 * rep:16 * rep + 16, :], in_=glx[pe][0:16, :]),
                        reads=[f"glx{pe}"], writes=[f"glx{pe}"], channel=f"glr2{pe}",
                        tag=f"glrp{e_}{rep}")
                p.dma("sync", lambda e, e_=e_, pe=pe: e.dma_start(
                    out=gatec[pe][:],
                    in_=gate_h[e_ * R:(e_ + 1) * R, 0:1].rearrange("(c p) o -> p (c o)", p=128),
                    allow_slow_non_contiguous=True),
                    reads=["gate", f"gatec{pe}"], writes=[f"gatec{pe}"], channel=f"glr{pe}",
                    tag=f"gld{e_}")

            for step, (bb, tt) in enumerate(ORDER):
                emit_tile_p1(bb, tt, step)
                if step > 0:
                    pb, ptt = ORDER[step - 1]
                    emit_tile_p2(pb, ptt, step - 1)
                if tt == 1 and bb > 0:
                    emit_routing(bb - 1)
            lb, ltt = ORDER[-1]
            emit_tile_p2(lb, ltt, len(ORDER) - 1)
            emit_routing(B - 1)
            prep_lists(0)

            p.barrier()

        ctxB = contextlib.ExitStack()
        with ctxB:
            sbb, psb = mk(ctxB)
            GCH = 640 if R % 640 == 0 else min(R, 512)
            NGCH = R // GCH
            SG = 320 if GCH % 320 == 0 else GCH
            assert R % GCH == 0 and GCH < 1024 and GCH % SG == 0 and SG <= 512
            xnTe = [sbb(f"xnTe{i}", [128, NGCH, FB, GCH], BF16) for i in range(2)]
            w1g = [sbb(f"w1g{i}", [128, FB, 128], BF16) for i in range(4)]
            w2sb = sbb("w2sb", [128, HB, C], BF16)
            hT = sbb("hT", [128, HB, R], BF16)
            ysb = [sbb(f"ysb{i}", [128, C], BF16) for i in range(2)]
            b2r = [sbb(f"b2r{i}", [1, C], BF16) for i in range(2)]
            psH = [psb(f"psH{i}", [128, 512]) for i in range(2)]
            psY = [psb(f"psY{i}", [128, C]) for i in range(2)]

            def prep_expert(e_):
                pe = e_ % 2
                if e_ > 0:
                    prep_lists(e_)
                for gq in range(NGCH):
                    r0 = gq * GCH
                    p.dma("gpsimd", lambda e, pe=pe, gq=gq: e.dma_gather(
                        out_ap=xnTe[pe][:, gq, :, :], in_ap=xn_rows,
                        idxs_ap=glx[pe][:, gq * (GCH // 16):(gq + 1) * (GCH // 16)],
                        num_idxs=GCH, num_idxs_reg=GCH, elem_size=C, transpose=True),
                        reads=["xn_rows", f"glx{pe}", f"xnTe{pe}"], writes=[f"xnTe{pe}"],
                        channel=f"gth{pe}", tag=f"gth{e_}_{gq}")

            def load_w2_chunk(e_, ck):
                hb0 = ck * (HB // 8)
                hb1 = (ck + 1) * (HB // 8)
                p.dma("gpsimd", lambda e, e_=e_, hb0=hb0, hb1=hb1: e.dma_start(
                    out=w2sb[:, hb0:hb1, :],
                    in_=w2_h[e_].rearrange("(hb p) c -> p hb c", p=128)[:, hb0:hb1, :]),
                    reads=["w2sb"], writes=["w2sb"], channel="w2l", tag=f"w2{e_}_{ck}")

            def load_w2(e_):
                load_w2_chunk(e_, 0)
                p.dma("gpsimd", lambda e, e_=e_: e.dma_start(
                    out=b2r[e_ % 2][:], in_=b2_h[e_].unsqueeze(0)),
                    reads=[f"b2r{e_ % 2}"], writes=[f"b2r{e_ % 2}"], channel="bl",
                    tag=f"b2{e_}")

            def load_w1g(e_, hb, buf):
                p.dma("gpsimd", lambda e, e_=e_, hb=hb, buf=buf: e.dma_start(
                    out=w1g[buf][:],
                    in_=w1_h[e_].rearrange("(fb p) h -> p fb h", p=128)[:, :, hb * 128:(hb + 1) * 128]),
                    reads=[f"w1g{buf}"], writes=[f"w1g{buf}"], channel=f"w1l{buf}",
                    tag=f"w1{e_}_{hb}")

            p.op("vector", lambda e: e.memset(ysb[0][:], 0.0), writes=["ysb0"], tag="zy")
            p.dma("scalar", lambda e: e.dma_start(out=y_store[ER:ER + 1, :], in_=ysb[0][0:1, :]),
                  reads=["ysb0"], writes=["y_store"], channel="ys0", tag="zyr")

            prep_expert(0)
            for j in range(3):
                if j < E * HB:
                    load_w1g(j // HB, j % HB, j % 4)
            for e_ in range(E):
                pe = e_ % 2
                load_w2(e_)
                if e_ + 1 < E:
                    prep_expert(e_ + 1)
                for hb in range(HB):
                    nh = e_ * HB + hb + 3
                    if nh < E * HB:
                        load_w1g(nh // HB, nh % HB, nh % 4)
                    step = max(1, HB // 8)
                    if (hb + 1) % step == 0 and (hb + 1) // step < 8:
                        load_w2_chunk(e_, (hb + 1) // step)
                    for rg in range(R // SG):
                        gq, so = divmod(rg * SG, GCH)
                        p_h = psH[rg % 2]
                        for fb in range(FB):
                            p.op("tensor", lambda e, hb=hb, rg=rg, fb=fb, pe=pe, gq=gq, so=so, p_h=p_h: e.matmul(
                                p_h[:, 0:SG], w1g[(e_ * HB + hb) % 4][:, fb, :],
                                xnTe[pe][:, gq, fb, so:so + SG],
                                start=(fb == 0), stop=(fb == FB - 1)),
                                reads=[f"w1g{(e_ * HB + hb) % 4}", f"xnTe{pe}"], writes=[f"psH{rg % 2}"],
                                tag=f"h{e_}_{hb}_{rg}_{fb}")
                        p.op("scalar", lambda e, hb=hb, rg=rg, e_=e_, p_h=p_h: e.activation(
                            hT[:, hb, rg * SG:(rg + 1) * SG], p_h[:, 0:SG], AF.Relu,
                            bias=b1all[:, e_ * HB + hb:e_ * HB + hb + 1], scale=1.0),
                            reads=[f"psH{rg % 2}", "b1all"], writes=["hT"],
                            tag=f"he{e_}_{hb}_{rg}")
                NH = C // 512
                for tc in range(RT):
                    for hf in range(NH):
                        for hb in range(HB):
                            p.op("tensor", lambda e, tc=tc, hb=hb, hf=hf: e.matmul(
                                psY[tc % 2][:, hf * 512:(hf + 1) * 512],
                                hT[:, hb, tc * 128:(tc + 1) * 128],
                                w2sb[:, hb, hf * 512:(hf + 1) * 512],
                                start=(hb == 0), stop=False, skip_group_check=True),
                                reads=["hT", "w2sb"], writes=[f"psY{tc % 2}"],
                                tag=f"ya{e_}_{tc}_{hf}_{hb}")
                        p.op("tensor", lambda e, tc=tc, pe=pe, hf=hf: e.matmul(
                            psY[tc % 2][:, hf * 512:(hf + 1) * 512], onesK1b[:],
                            b2r[pe][0:1, hf * 512:(hf + 1) * 512],
                            start=False, stop=True, skip_group_check=True),
                            reads=["onesK1b", f"b2r{pe}"], writes=[f"psY{tc % 2}"],
                            tag=f"yb{e_}_{tc}_{hf}")
                    p.op("scalar", lambda e, tc=tc, pe=pe: e.activation(
                        ysb[tc % 2][:], psY[tc % 2][:], AF.Identity,
                        scale=gatec[pe][:, tc:tc + 1]),
                        reads=[f"psY{tc % 2}", f"gatec{pe}"], writes=[f"ysb{tc % 2}"],
                        tag=f"ye{e_}_{tc}")
                    p.dma("scalar", lambda e, e_=e_, tc=tc: e.dma_start(
                        out=y_store[e_ * R + tc * 128: e_ * R + (tc + 1) * 128, :],
                        in_=ysb[tc % 2][:]),
                        reads=[f"ysb{tc % 2}"], writes=["y_store"], channel=f"ys{tc % 2}",
                        tag=f"yst{e_}_{tc}")

            p.barrier()

        ctxC = contextlib.ExitStack()
        with ctxC:
            sbc, psc = mk(ctxC)
            gth = [[sbc(f"gthc{k}_{i}", [128, NT * C], BF16) for i in range(2)]
                   for k in range(2)]
            xtc = [sbc(f"xtc{i}", [128, C], F32) for i in range(2)]
            stc = [sbc(f"stc{i}", [128, C], BF16) for i in range(2)]
            gpre = [sbc(f"gpre{i}", [128, C], F32) for i in range(2)]
            otc = [sbc(f"otc{i}", [128, C], F32) for i in range(2)]

            for bb in range(B):
                bp = bb % 2
                YCH = min(T, 1024)
                NYCH = T // YCH
                for k in range(2):
                    for yq in range(NYCH):
                        p.dma("gpsimd", lambda e, bb=bb, k=k, bp=bp, yq=yq: e.dma_gather(
                            out_ap=gth[k][bp][:].rearrange("p (n c) -> p n c", n=NT)[
                                :, yq * (YCH // 128):(yq + 1) * (YCH // 128), :],
                            in_ap=y_store,
                            idxs_ap=ydix[bb][k][:, yq * (YCH // 16):(yq + 1) * (YCH // 16)],
                            num_idxs=YCH, num_idxs_reg=YCH, elem_size=C, transpose=False),
                            reads=["y_store", f"ydix{bb}_{k}", f"gthc{k}_{bp}"],
                            writes=[f"gthc{k}_{bp}"], channel=f"gbk{k}", tag=f"gb{bb}{k}_{yq}")
                for tc in range(NT):
                    pc = tc % 2
                    p.dma("sync", lambda e, bb=bb, tc=tc, pc=pc: e.dma_start(
                        out=xtc[pc][:], in_=x_in[bb, tc * 128:(tc + 1) * 128, :]),
                        reads=[f"xtc{pc}"], writes=[f"xtc{pc}"], channel=f"xc{pc}",
                        tag=f"xc{bb}_{tc}")
                    p.dma("sync", lambda e, bb=bb, tc=tc, pc=pc: e.dma_start(
                        out=stc[pc][:], in_=s_rows[bb * T + tc * 128: bb * T + (tc + 1) * 128, :]),
                        reads=["s_rows", f"stc{pc}"], writes=[f"stc{pc}"], channel=f"scl{pc}",
                        tag=f"scl{bb}_{tc}")
                    p.op("vector", lambda e, tc=tc, pc=pc, bp=bp: e.tensor_tensor(
                        gpre[pc][:], gth[0][bp][:, tc * C:(tc + 1) * C],
                        gth[1][bp][:, tc * C:(tc + 1) * C], ALU.add),
                        reads=[f"gthc0_{bp}", f"gthc1_{bp}"], writes=[f"gpre{pc}"],
                        tag=f"gaa{bb}_{tc}")
                    p.op("scalar", lambda e, pc=pc: e.activation(gpre[pc][:], gpre[pc][:], AF.Silu),
                         reads=[f"gpre{pc}"], writes=[f"gpre{pc}"], tag=f"si{bb}_{tc}")
                    p.op("vector", lambda e, pc=pc: e.tensor_tensor(
                        gpre[pc][:], stc[pc][:], gpre[pc][:], ALU.mult),
                        reads=[f"stc{pc}", f"gpre{pc}"], writes=[f"gpre{pc}"], tag=f"sg{bb}_{tc}")
                    p.op("vector", lambda e, pc=pc: e.tensor_tensor(
                        otc[pc][:], xtc[pc][:], gpre[pc][:], ALU.add),
                        reads=[f"xtc{pc}", f"gpre{pc}"], writes=[f"otc{pc}"], tag=f"ob{bb}_{tc}")
                    p.dma("sync", lambda e, bb=bb, tc=tc, pc=pc: e.dma_start(
                        out=out_h[bb, tc * 128:(tc + 1) * 128, :], in_=otc[pc][:]),
                        reads=[f"otc{pc}"], writes=["outw"], channel=f"os{pc}",
                        tag=f"os{bb}_{tc}")

            p.op("sync", lambda e: e.nop(), reads=["outw"], tag="fin")

        with nc.Block() as blk:
            p.build(blk, ctx)
    return nc


def kernel(L0_attn_residual, L0_attn_w_fc, L0_attn_w_proj, L0_attn_ln_g,
           L0_attn_ln_b, L0_ffn_Wr, L0_ffn_W1, L0_ffn_b1, L0_ffn_W2, L0_ffn_b2):
    cfg = default_cfg()
    n_cores = 8
    B = cfg["B"]
    x = np.ascontiguousarray(np.asarray(L0_attn_residual, dtype=np.float32))
    shared = {
        "w_fc": np.ascontiguousarray(np.asarray(L0_attn_w_fc, dtype=np.float32)),
        "w_proj": np.ascontiguousarray(np.asarray(L0_attn_w_proj, dtype=np.float32)),
        "ln_g": np.ascontiguousarray(np.asarray(L0_attn_ln_g, dtype=np.float32)),
        "ln_b": np.ascontiguousarray(np.asarray(L0_attn_ln_b, dtype=np.float32)),
        "wr": np.ascontiguousarray(np.asarray(L0_ffn_Wr, dtype=np.float32)),
        "w1": np.ascontiguousarray(np.asarray(L0_ffn_W1, dtype=np.float32)),
        "b1": np.ascontiguousarray(np.asarray(L0_ffn_b1, dtype=np.float32)),
        "w2": np.ascontiguousarray(np.asarray(L0_ffn_W2, dtype=np.float32)),
        "b2": np.ascontiguousarray(np.asarray(L0_ffn_b2, dtype=np.float32)),
    }
    nc = build_nc(cfg)
    if not nc.is_finalized():
        nc.finalize()
    in_maps = []
    for c in range(n_cores):
        m = dict(shared)
        m["x"] = np.ascontiguousarray(x[c * B:(c + 1) * B])
        in_maps.append(m)
    res = run_bass_kernel_spmd(nc, in_maps, core_ids=list(range(n_cores)))
    out = np.concatenate([np.asarray(r["out"]) for r in res.results], axis=0)
    return out.astype(np.float32)


# version 20
# speedup vs baseline: 1.0902x; 1.0902x over previous
import contextlib
from collections import defaultdict

import numpy as np

import concourse.bass as bass
import concourse.bacc as bacc
import concourse.mybir as mybir
from concourse.bass_utils import run_bass_kernel_spmd

F32 = mybir.dt.float32
BF16 = mybir.dt.bfloat16
I16 = mybir.dt.int16
AF = mybir.ActivationFunctionType
ALU = mybir.AluOpType
AX = mybir.AxisListType

ENGINES = ("tensor", "vector", "scalar", "gpsimd", "sync")
BIGC = 65536.0
EPS = 1e-5


class _Op:
    __slots__ = ("engine", "fn", "reads", "writes", "is_dma", "seq",
                 "sem", "inc_val", "post_val", "waits", "clock", "tag")

    def __init__(self, engine, fn, reads, writes, is_dma, tag=""):
        self.engine = engine
        self.fn = fn
        self.reads = reads
        self.writes = writes
        self.is_dma = is_dma
        self.tag = tag
        self.sem = None
        self.inc_val = 0
        self.post_val = 0
        self.waits = {}
        self.clock = {}


class Prog:
    def __init__(self, nc):
        self.nc = nc
        self.ops = []
        self.last_write = {}
        self.readers = defaultdict(list)
        self.eng_clock = {eng: {} for eng in ENGINES}
        self.sem_counts = defaultdict(int)
        self.dma_channels = set()
        self.sem_objs = {}

    def op(self, engine, fn, reads=(), writes=(), tag=""):
        return self._op(engine, fn, reads, writes, False, None, tag)

    def dma(self, engine, fn, reads=(), writes=(), channel=None, tag=""):
        ch = channel or f"dma_{engine}"
        return self._op(engine, fn, reads, writes, True, ch, tag)

    def _op(self, engine, fn, reads, writes, is_dma, channel, tag):
        o = _Op(engine, fn, tuple(reads), tuple(writes), is_dma, tag)
        o.seq = len(self.ops)
        if is_dma:
            o.sem = f"ch_{channel}"
            o.inc_val = 16
            self.dma_channels.add(o.sem)
        else:
            o.sem = f"eng_{engine}"
            o.inc_val = 1

        deps = []
        for k in o.reads:
            w = self.last_write.get(k)
            if w is not None:
                deps.append(w)
        for k in o.writes:
            w = self.last_write.get(k)
            if w is not None:
                deps.append(w)
            deps.extend(self.readers.get(k, ()))

        need = {}
        for d in deps:
            v = self.sem_counts[d.sem] if d.sem in self.dma_channels else d.post_val
            if need.get(d.sem, 0) < v:
                need[d.sem] = v
        if is_dma:
            prev = self.sem_counts[o.sem]
            if prev > 0 and need.get(o.sem, 0) < prev:
                need[o.sem] = prev

        clk = self.eng_clock[engine]
        o.waits = {s: v for s, v in need.items() if clk.get(s, 0) < v}

        self.sem_counts[o.sem] += o.inc_val
        o.post_val = self.sem_counts[o.sem]

        newclk = dict(clk)
        if engine == "tensor" and not is_dma:
            newclk[o.sem] = o.post_val
        for s, v in need.items():
            if newclk.get(s, 0) < v:
                newclk[s] = v
        for d in deps:
            for s, v in d.clock.items():
                if s in self.dma_channels:
                    continue
                if newclk.get(s, 0) < v:
                    newclk[s] = v
        self.eng_clock[engine] = newclk
        o.clock = dict(newclk)
        o.clock[o.sem] = o.post_val

        for k in o.writes:
            self.last_write[k] = o
            self.readers[k] = []
        for k in o.reads:
            self.readers[k].append(o)
        self.ops.append(o)
        return o

    def barrier(self):
        totals = dict(self.sem_counts)
        for eng in ENGINES:
            clk = self.eng_clock[eng]
            waits = {s: v for s, v in totals.items() if v > 0 and clk.get(s, 0) < v}
            o = _Op(eng, lambda e: e.nop(), (), (), False, f"bar_{eng}")
            o.seq = len(self.ops)
            o.sem = f"eng_{eng}"
            o.inc_val = 1
            o.waits = waits
            self.sem_counts[o.sem] += 1
            o.post_val = self.sem_counts[o.sem]
            newclk = dict(clk)
            for s, v in totals.items():
                if newclk.get(s, 0) < v:
                    newclk[s] = v
            self.eng_clock[eng] = newclk
            o.clock = dict(newclk)
            o.clock[o.sem] = o.post_val
            self.ops.append(o)

    def build(self, block, exitstack):
        nc = self.nc
        for name in sorted(self.sem_counts):
            self.sem_objs[name] = exitstack.enter_context(nc.semaphore(name))
        needed = defaultdict(int)
        for o in self.ops:
            for s, v in o.waits.items():
                if needed[s] < v:
                    needed[s] = v
        per_engine = defaultdict(list)
        for o in self.ops:
            per_engine[o.engine].append(o)

        def emit_stream(eng_name, eng):
            for o in per_engine[eng_name]:
                for s, v in sorted(o.waits.items()):
                    eng.wait_ge(self.sem_objs[s], v)
                inst = o.fn(eng)
                if o.is_dma or needed[o.sem] >= o.post_val:
                    if inst is None:
                        raise RuntimeError(f"op {o.tag or o.seq} needs inc but returned None")
                    inst.then_inc(self.sem_objs[o.sem], o.inc_val)

        for eng_name in ENGINES:
            if not per_engine[eng_name]:
                continue
            getattr(block, eng_name)(lambda eng, en=eng_name: emit_stream(en, eng))


def default_cfg():
    return dict(B=4, T=2048, C=1024, E=8, CAP=320, H=4096, QH=4)


def build_nc(cfg):
    B, T, C, E, CAP, H, QH = (cfg["B"], cfg["T"], cfg["C"], cfg["E"],
                              cfg["CAP"], cfg["H"], cfg["QH"])
    D = C // QH
    NS = C * C // D
    SC = NS // 128
    FB = C // 128
    HB = H // 128
    DB = D // 128
    NT = T // 128
    S = 2 * T
    R = B * CAP
    assert R % 128 == 0 and T % 16 == 0 and S % 512 == 0
    ER = E * R
    DUMP = ER
    RT = R // 128
    GRP = min(512, T)
    TPG = GRP // 128

    nc = bacc.Bacc("TRN2", target_bir_lowering=False)
    x_in = nc.dram_tensor("x", [B, T, C], F32, kind="ExternalInput").ap()
    w_fc = nc.dram_tensor("w_fc", [C, C], F32, kind="ExternalInput").ap()
    w_proj = nc.dram_tensor("w_proj", [C, C], F32, kind="ExternalInput").ap()
    ln_g = nc.dram_tensor("ln_g", [C], F32, kind="ExternalInput").ap()
    ln_b = nc.dram_tensor("ln_b", [C], F32, kind="ExternalInput").ap()
    wr_h = nc.dram_tensor("wr", [C, E], F32, kind="ExternalInput").ap()
    w1_h = nc.dram_tensor("w1", [E, C, H], F32, kind="ExternalInput").ap()
    b1_h = nc.dram_tensor("b1", [E, H], F32, kind="ExternalInput").ap()
    w2_h = nc.dram_tensor("w2", [E, H, C], F32, kind="ExternalInput").ap()
    b2_h = nc.dram_tensor("b2", [E, C], F32, kind="ExternalInput").ap()
    out_h = nc.dram_tensor("out", [B, T, C], F32, kind="ExternalOutput").ap()

    N = B * T
    xn_rows = nc.dram_tensor("xn_rows", [N, C], BF16, kind="Internal").ap()
    s_rows = nc.dram_tensor("s_rows", [N, C], BF16, kind="Internal").ap()
    y_store = nc.dram_tensor("y_store", [ER + 1, C], BF16, kind="Internal").ap()
    ef_h = nc.dram_tensor("ef_h", [B, T, 2], F32, kind="Internal").ap()
    idxq_h = nc.dram_tensor("idxq_h", [B, S], I16, kind="Internal").ap()
    glist_h = nc.dram_tensor("glist_h", [ER + 1, 64], F32, kind="Internal").ap()
    gate_h = nc.dram_tensor("gate_h", [ER + 1, 64], F32, kind="Internal").ap()

    p = Prog(nc)
    ctx = contextlib.ExitStack()

    with ctx:
        def mk(stack):
            def sb(name, shape, dt):
                return stack.enter_context(nc.sbuf_tensor(name, shape, dt))

            def ps(name, shape):
                return stack.enter_context(nc.psum_tensor(name, shape, F32))
            return sb, ps

        sbg, psg = mk(ctx)
        identF = sbg("identF", [128, 128], F32)
        onesK1b = sbg("onesK1b", [1, 128], BF16)
        Rg = cfg["B"] * cfg["CAP"]
        glf = sbg("glf", [16, Rg // 16], F32)
        glx = [sbg(f"glx{i}", [128, Rg // 16], I16) for i in range(2)]
        gatec = [sbg(f"gatec{i}", [128, Rg // 128], F32) for i in range(2)]
        ydix = [[sbg(f"ydix{b_}_{k_}", [128, cfg["T"] // 16], I16) for k_ in range(2)]
                for b_ in range(cfg["B"])]
        b1all = sbg("b1all", [128, E * HB], F32)

        ctxA = contextlib.ExitStack()
        with ctxA:
            sba, psa = mk(ctxA)
            ones18 = sba("ones18", [1, 8], F32)
            ones81 = sba("ones81", [8, 1], F32)
            iotaR = sba("iotaR", [128, E], F32)
            iotaRB = sba("iotaRB", [128, E], F32)
            iotaE8 = sba("iotaE8", [8, 1], F32)
            colER = sba("colER", [8, 1], F32)
            valw = sba("valw", [128, B * NT], F32)
            onesS = sba("onesS", [8, S], BF16)
            zer = sba("zer", [128, 64], F32)
            epsc = sba("epsc", [128, 1], F32)
            gcol = sba("gcol", [128, FB], F32)
            bcol = sba("bcol", [128, FB], F32)
            wr_sb = sba("wr_sb", [128, FB, E], F32)
            sWr = sba("sWr", [128, E], F32)
            kT = sba("kT", [128, DB, NS], BF16)
            v_sb = sba("v_sb", [128, SC, D], BF16)

            p.op("gpsimd", lambda e: e.iota(identF[:], [[1, 128]], channel_multiplier=-1,
                                            allow_small_or_imprecise_dtypes=True),
                 writes=["identF"], tag="io_id")
            p.op("gpsimd", lambda e: e.iota(iotaR[:], [[1, E]], channel_multiplier=0,
                                            allow_small_or_imprecise_dtypes=True),
                 writes=["iotaR"], tag="io_r")
            p.op("gpsimd", lambda e: e.iota(iotaE8[:], [[0, 1]], channel_multiplier=1,
                                            allow_small_or_imprecise_dtypes=True),
                 writes=["iotaE8"], tag="io_e")
            for bb in range(B):
                for tt in range(NT):
                    j = bb * NT + tt
                    p.op("gpsimd", lambda e, j=j, bb=bb, tt=tt: e.iota(
                        valw[:, j:j + 1], [[0, 1]], base=bb * T + tt * 128,
                        channel_multiplier=1, allow_small_or_imprecise_dtypes=True),
                        writes=["valw"], tag=f"io_v{j}")

            p.op("vector", lambda e: e.tensor_scalar(identF[:], identF[:], 0.0, None, ALU.is_equal),
                 reads=["identF"], writes=["identF"], tag="mkident")
            p.op("vector", lambda e: e.tensor_scalar(iotaRB[:], iotaR[:], -BIGC, None, ALU.add),
                 reads=["iotaR"], writes=["iotaRB"])
            p.op("vector", lambda e: e.tensor_scalar(colER[:], iotaE8[:], float(R), None, ALU.mult),
                 reads=["iotaE8"], writes=["colER"])
            p.op("vector", lambda e: e.memset(ones18[:], 1.0), writes=["ones18"])
            p.op("vector", lambda e: e.memset(ones81[:], 1.0), writes=["ones81"])
            p.op("vector", lambda e: e.memset(onesK1b[:], 1.0), writes=["onesK1b"])
            p.op("vector", lambda e: e.memset(onesS[:], 1.0), writes=["onesS"])
            p.op("vector", lambda e: e.memset(zer[:], 0.0), writes=["zer"])
            p.op("vector", lambda e: e.memset(epsc[:], EPS), writes=["epsc"])

            nzb = (ER + 1 + 127) // 128
            for i in range(nzb):
                r0, r1 = i * 128, min((i + 1) * 128, ER + 1)
                p.dma("sync", lambda e, r0=r0, r1=r1: e.dma_start(
                    out=glist_h[r0:r1, :], in_=zer[0:r1 - r0, :]),
                    reads=["zer"], writes=["glist"], channel="zg", tag=f"zg{i}")
                p.dma("sync", lambda e, r0=r0, r1=r1: e.dma_start(
                    out=gate_h[r0:r1, :], in_=zer[0:r1 - r0, :]),
                    reads=["zer"], writes=["gate"], channel="zg", tag=f"zp{i}")

            p.dma("sync", lambda e: e.dma_start(
                out=gcol[:], in_=ln_g.rearrange("(f p) -> p f", p=128), allow_slow_non_contiguous=True),
                writes=["gcol"], channel="cst", tag="lg")
            p.dma("sync", lambda e: e.dma_start(
                out=bcol[:], in_=ln_b.rearrange("(f p) -> p f", p=128), allow_slow_non_contiguous=True),
                writes=["bcol"], channel="cst", tag="lb")
            p.dma("sync", lambda e: e.dma_start(
                out=wr_sb[:], in_=wr_h.rearrange("(f p) e -> p f e", p=128), allow_slow_non_contiguous=True),
                writes=["wr_sb"], channel="cst", tag="lwr")
            p.dma("sync", lambda e: e.dma_start(
                out=b1all[:], in_=b1_h.rearrange("e (h p) -> p (e h)", p=128), allow_slow_non_contiguous=True),
                writes=["b1all"], channel="cst", tag="lb1")

            with contextlib.ExitStack() as c0:
                sb0, ps0 = mk(c0)
                stg = sb0("stg", [128, FB, C], F32)
                tbf = sb0("tbf", [128, FB, C], BF16)
                onesPE = sb0("onesPE", [128, 128], F32)
                psT0 = ps0("psT0", [128, 1024])
                psW = ps0("psW", [128, 512])
                p.op("vector", lambda e: e.memset(onesPE[:], 1.0), writes=["onesPE"])
                p.op("vector", lambda e: e.memset(v_sb[:], 0.0), writes=["v_sb"])
                p.op("vector", lambda e: e.memset(kT[:], 0.0), writes=["kT"])

                for wname, hbuf in (("fc", w_fc), ("pj", w_proj)):
                    p.dma("sync", lambda e, hbuf=hbuf: e.dma_start(
                        out=stg[:], in_=hbuf.rearrange("(a p) c -> p a c", p=128)),
                        reads=["stg"], writes=["stg"], channel="cst", tag=f"lw{wname}")
                    for rb in range(FB):
                        for cb in range(FB):
                            par = (rb * FB + cb) % 2
                            pk_ = ("psT0", par)
                            p.op("tensor", lambda e, rb=rb, cb=cb, par=par: e.transpose(
                                psT0[:, par * 512:par * 512 + 128],
                                stg[:, rb, cb * 128:(cb + 1) * 128], identF[:]),
                                reads=["stg", "identF"], writes=[pk_], tag=f"T{wname}{rb}{cb}")
                            p.op("vector", lambda e, rb=rb, cb=cb, par=par: e.tensor_copy(
                                tbf[:, cb, rb * 128:(rb + 1) * 128],
                                psT0[:, par * 512:par * 512 + 128]),
                                reads=[pk_], writes=["tbf"], tag=f"E{wname}{rb}{cb}")
                    if wname == "fc":
                        for db in range(DB):
                            for sr in range(4):
                                cb = (sr * D + db * 128) // 128
                                p.op("vector", lambda e, db=db, sr=sr, cb=cb: e.tensor_copy(
                                    kT[:, db, sr::4], tbf[:, cb, 0:NS // 4]),
                                    reads=["tbf"], writes=["kT"], tag=f"k{db}{sr}")
                        for fb in range(FB):
                            p.op("tensor", lambda e, fb=fb: e.matmul(
                                psW[:, 0:E], onesPE[:], wr_sb[:, fb, :],
                                start=(fb == 0), stop=(fb == FB - 1)),
                                reads=["onesPE", "wr_sb"], writes=["psW"], tag=f"sw{fb}")
                        p.op("vector", lambda e: e.tensor_copy(sWr[:], psW[:, 0:E]),
                             reads=["psW"], writes=["sWr"])
                    else:
                        for q in range(4):
                            for w in range(4):
                                p.dma("scalar", lambda e, q=q, w=w: e.dma_start(
                                    out=v_sb[q:q + 125:4, :, :].rearrange(
                                        "p (u w2) d -> p u w2 d", w2=4)[:, :, w, :],
                                    in_=tbf[32 * w:32 * w + 32, :, q * D:(q + 1) * D]),
                                    reads=["tbf"], writes=["v_sb"], channel="vsh",
                                    tag=f"v{q}{w}")
                p.barrier()
            sWr_key = "sWr"

            xt = [sba(f"xt{i}", [128, C], F32) for i in range(3)]
            st6_ = [sba(f"st6{i}", [128, 12], F32) for i in range(2)]
            mv_ = [sba(f"mv{i}", [128, 2], F32) for i in range(2)]
            sc1_ = [sba(f"sc1{i}", [128, 4], F32) for i in range(2)]
            sd2_ = [sba(f"sd2{i}", [128, 1], F32) for i in range(2)]
            mu2_ = [sba(f"mu2{i}", [128, 1], F32) for i in range(2)]
            r2c_ = [sba(f"r2c{i}", [128, 2], F32) for i in range(2)]
            xn_f = [sba(f"xn_f{i}", [128, C], F32) for i in range(2)]
            xnb = [sba(f"xnb{i}", [128, C], BF16) for i in range(2 * TPG)]
            ap_f = [sba(f"ap_f{i}", [128, C], F32) for i in range(2)]
            apT = [sba(f"apT{i}", [128, C], F32) for i in range(2)]
            aiT_ = [sba(f"aiT{i}", [128, FB, GRP], BF16) for i in range(2)]
            pT = [sba(f"pT{i}", [128, GRP], BF16) for i in range(2)]
            uT = [sba(f"uT{i}", [128, GRP], BF16) for i in range(2)]
            sev = [sba(f"sev{i}", [128, C], BF16) for i in range(TPG)]
            lgt_ = [sba(f"lgt{i}", [128, E], F32) for i in range(2)]
            eqt_ = [sba(f"eqt{i}", [128, E], F32) for i in range(2)]
            cnd_ = [sba(f"cnd{i}", [128, E], F32) for i in range(2)]
            msk_ = [sba(f"msk{i}", [128, E], F32) for i in range(2)]
            exq_ = [sba(f"exq{i}", [128, E], F32) for i in range(2)]
            mm1_ = [sba(f"mm1{i}", [128, 1], F32) for i in range(2)]
            mm2t_ = [sba(f"mm2t{i}", [128, 1], F32) for i in range(2)]
            dent_ = [sba(f"dent{i}", [128, 1], F32) for i in range(2)]
            ipack_ = [sba(f"ipack{i}", [128, 2], F32) for i in range(2)]
            pk0sb = [sba(f"pk0sb{i}", [128, NT], F32) for i in range(B)]
            pk1sb = [sba(f"pk1sb{i}", [128, NT], F32) for i in range(B)]
            gA = sba("gA", [8, S], F32)
            gB = sba("gB", [8, S], F32)
            gC = sba("gC", [8, S], F32)
            efr = sba("efr", [1, S], F32)
            idxi = sba("idxi", [1, S], I16)
            idxw_ = [sba(f"idxw{i}", [128, T // 16], I16) for i in range(2)]

            psT = [psa("psTs0", [128, 512])]
            psLG = psa("psLG", [128, 512])
            psSC = [psa(f"psSC{i}", [128, 512]) for i in range(2)]
            psO = psa("psO", [128, TPG * 512])

            ORDER = [(bb, tt) for bb in range(B) for tt in range(NT)]

            def xload(step):
                if step >= len(ORDER):
                    return
                bb, tt = ORDER[step]
                p.dma("sync", lambda e, bb=bb, tt=tt, step=step: e.dma_start(
                    out=xt[step % 3][:], in_=x_in[bb, tt * 128:(tt + 1) * 128, :]),
                    reads=[f"xt{step % 3}"], writes=[f"xt{step % 3}"],
                    channel=f"xl{step % 3}", tag=f"xl{step}")

            xload(0)
            xload(1)
            xload(2)

            def emit_tile_p1(bb, tt, estep):
                    idx = bb * NT + tt
                    gpar = (idx // TPG) % 2
                    tp_ = estep % 2
                    st6 = st6_[tp_]
                    mv = mv_[tp_]
                    sc1 = sc1_[tp_]
                    sd2 = sd2_[tp_]
                    mu2 = mu2_[tp_]
                    r2c = r2c_[tp_]
                    lgt = lgt_[tp_]
                    eqt = eqt_[tp_]
                    cnd = cnd_[tp_]
                    msk = msk_[tp_]
                    exq = exq_[tp_]
                    mm1 = mm1_[tp_]
                    mm2t = mm2t_[tp_]
                    dent = dent_[tp_]
                    ipack = ipack_[tp_]
                    bpar = bb
                    aiT = aiT_[gpar]
                    X = xt[estep % 3]
                    xk = f"xt{estep % 3}"
                    XN = xn_f[estep % 2]
                    xnk = f"xn_f{estep % 2}"
                    XNB = xnb[gpar * TPG + tt % TPG]
                    xnbk = f"xnb{gpar * TPG + tt % TPG}"
                    AP_ = ap_f[estep % 2]
                    apk = f"ap_f{estep % 2}"
                    APT = apT[estep % 2]
                    aptk = f"apT{estep % 2}"

                    p.op("vector", lambda e, X=X: e.bn_stats(st6[:, 0:6], X[:, 0:C // 2]),
                         reads=[xk], writes=[f"st6{tp_}"], tag=f"bs1a{idx}")
                    p.op("vector", lambda e, X=X: e.bn_stats(st6[:, 6:12], X[:, C // 2:C]),
                         reads=[xk], writes=[f"st6{tp_}"], tag=f"bs1b{idx}")
                    p.op("vector", lambda e: e.bn_aggr(mv[:], st6[:]),
                         reads=[f"st6{tp_}"], writes=[f"mv{tp_}"], tag=f"ba1{idx}")
                    p.op("scalar", lambda e: e.activation(sc1[:, 0:1], mv[:, 1:2], AF.Sqrt, bias=epsc[:]),
                         reads=[f"mv{tp_}", "epsc"], writes=[f"sc1{tp_}"], tag=f"sq1{idx}")
                    p.op("vector", lambda e: e.reciprocal(sc1[:, 1:2], sc1[:, 0:1]),
                         reads=[f"sc1{tp_}"], writes=[f"sc1{tp_}"], tag=f"rc1{idx}")
                    p.op("vector", lambda e: e.scalar_tensor_tensor(
                        sc1[:, 2:3], mv[:, 0:1], -1.0, sc1[:, 1:2], ALU.mult, ALU.mult),
                        reads=[f"mv{tp_}", f"sc1{tp_}"], writes=[f"sc1{tp_}"], tag=f"b1c{idx}")
                    p.op("scalar", lambda e, X=X, XN=XN: e.activation(
                        XN[:], X[:], AF.Identity, bias=sc1[:, 2:3], scale=sc1[:, 1:2]),
                        reads=[xk, f"sc1{tp_}"], writes=[xnk], tag=f"xn{idx}")
                    p.op("scalar", lambda e, XN=XN, XNB=XNB: e.activation(
                        XNB[:], XN[:], AF.Copy, bias=0.0, scale=1.0),
                        reads=[xnk], writes=[xnbk], tag=f"xnbc{idx}")
                    p.dma("scalar", lambda e, XNB=XNB, bb=bb, tt=tt: e.dma_start(
                        out=xn_rows[bb * T + tt * 128: bb * T + (tt + 1) * 128, :], in_=XNB[:]),
                        reads=[xnbk], writes=["xn_rows"], channel=f"xs{estep % 2}", tag=f"xns{idx}")
                    p.op("vector", lambda e, XN=XN: e.bn_stats(st6[:, 0:6], XN[:, 0:C // 2]),
                         reads=[xnk], writes=[f"st6{tp_}"], tag=f"bs2a{idx}")
                    p.op("vector", lambda e, XN=XN: e.bn_stats(st6[:, 6:12], XN[:, C // 2:C]),
                         reads=[xnk], writes=[f"st6{tp_}"], tag=f"bs2b{idx}")
                    p.op("vector", lambda e: e.bn_aggr(mv[:], st6[:]),
                         reads=[f"st6{tp_}"], writes=[f"mv{tp_}"], tag=f"ba2{idx}")
                    p.op("scalar", lambda e: e.activation(sd2[:], mv[:, 1:2], AF.Sqrt, bias=epsc[:]),
                         reads=[f"mv{tp_}", "epsc"], writes=[f"sd2{tp_}"], tag=f"sq2{idx}")
                    p.op("vector", lambda e: e.tensor_copy(mu2[:], mv[:, 0:1]),
                         reads=[f"mv{tp_}"], writes=[f"mu2{tp_}"], tag=f"mu2{idx}")
                    p.op("vector", lambda e: e.reciprocal(r2c[:, 0:1], sd2[:]),
                         reads=[f"sd2{tp_}"], writes=[f"r2c{tp_}"], tag=f"rc2{idx}")
                    p.op("vector", lambda e: e.scalar_tensor_tensor(
                        r2c[:, 1:2], mu2[:], -1.0, r2c[:, 0:1], ALU.mult, ALU.mult),
                        reads=[f"mu2{tp_}", f"r2c{tp_}"], writes=[f"r2c{tp_}"], tag=f"b2c{idx}")
                    p.op("scalar", lambda e, XN=XN, AP_=AP_: e.activation(
                        AP_[:], XN[:], AF.Identity, bias=r2c[:, 1:2], scale=r2c[:, 0:1]),
                        reads=[xnk, f"r2c{tp_}"], writes=[apk], tag=f"apa{idx}")
                    for fb in range(FB):
                        pk_ = ("psT", 0)
                        p.op("tensor", lambda e, AP_=AP_, fb=fb: e.transpose(
                            psT[0][:, 0:128],
                            AP_[:, fb * 128:(fb + 1) * 128], identF[:]),
                            reads=[apk, "identF"], writes=[pk_], tag=f"tr{idx}_{fb}")
                        p.op("vector", lambda e, fb=fb, APT=APT: e.tensor_copy(
                            APT[:, fb * 128:(fb + 1) * 128],
                            psT[0][:, 0:128]),
                            reads=[pk_], writes=[aptk], tag=f"ae{idx}_{fb}")
                        p.op("scalar", lambda e, fb=fb, tt=tt, APT=APT: e.activation(
                            aiT[:, fb, (tt % TPG) * 128:(tt % TPG + 1) * 128],
                            APT[:, fb * 128:(fb + 1) * 128],
                            AF.Identity, bias=bcol[:, fb:fb + 1], scale=gcol[:, fb:fb + 1]),
                            reads=[aptk, "gcol", "bcol"], writes=[f"aiT{gpar}"], tag=f"ai{idx}_{fb}")
                    for fb in range(FB):
                        p.op("tensor", lambda e, fb=fb, APT=APT: e.matmul(
                            psLG[:, 0:E], APT[:, fb * 128:(fb + 1) * 128], wr_sb[:, fb, :],
                            start=(fb == 0), stop=(fb == FB - 1)),
                            reads=[aptk, "wr_sb"], writes=["psLG"], tag=f"rt{idx}_{fb}")
                    p.op("vector", lambda e: e.tensor_scalar(lgt[:], psLG[:, 0:E], sd2[:], None, ALU.mult),
                         reads=["psLG", f"sd2{tp_}"], writes=[f"lgt{tp_}"], tag=f"lg1{idx}")
                    p.op("vector", lambda e: e.scalar_tensor_tensor(
                        lgt[:], sWr[:], mu2[:], lgt[:], ALU.mult, ALU.add),
                        reads=[sWr_key, f"mu2{tp_}", f"lgt{tp_}"], writes=[f"lgt{tp_}"], tag=f"lg2{idx}")

                    xload(estep + 3)

                    if tt % TPG == TPG - 1:
                        g0 = (tt // TPG) * GRP
                        for h in range(QH):
                            for scj in range(SC + 1):
                                if scj < SC:
                                    for db in range(DB):
                                        p.op("tensor", lambda e, h=h, scj=scj, db=db: e.matmul(
                                            psSC[scj % 2][:, 0:GRP],
                                            kT[:, db, scj * 128:(scj + 1) * 128],
                                            aiT[:, h * DB + db, :],
                                            start=(db == 0), stop=(db == DB - 1),
                                            skip_group_check=True),
                                            reads=["kT", f"aiT{gpar}"], writes=[f"psSC{scj % 2}"],
                                            tag=f"qk{idx}_{h}_{scj}_{db}")
                                if scj > 0:
                                    sc_ = scj - 1
                                    p.op("scalar", lambda e, sc_=sc_: e.activation(
                                        uT[sc_ % 2][:], psSC[sc_ % 2][:, 0:GRP], AF.Relu),
                                        reads=[f"psSC{sc_ % 2}"], writes=[f"uT{sc_ % 2}"],
                                        tag=f"ru_{idx}_{h}_{sc_}")
                                    p.op("vector", lambda e, sc_=sc_: e.tensor_tensor(
                                        pT[sc_ % 2][:], uT[sc_ % 2][:],
                                        psSC[sc_ % 2][:, 0:GRP], ALU.mult),
                                        reads=[f"uT{sc_ % 2}", f"psSC{sc_ % 2}"],
                                        writes=[f"pT{sc_ % 2}"],
                                        tag=f"r2_{idx}_{h}_{sc_}")
                                    for st4 in range(TPG):
                                        p.op("tensor", lambda e, sc_=sc_, st4=st4: e.matmul(
                                            psO[:, st4 * 512:st4 * 512 + D],
                                            pT[sc_ % 2][:, st4 * 128:(st4 + 1) * 128],
                                            v_sb[:, sc_, :],
                                            start=(sc_ == 0), stop=(sc_ == SC - 1),
                                            skip_group_check=True),
                                            reads=[f"pT{sc_ % 2}", "v_sb"],
                                            writes=[("psO", st4)],
                                            tag=f"pv{idx}_{h}_{sc_}_{st4}")
                            for st4 in range(TPG):
                                XB = xnb[gpar * TPG + st4]
                                p.op("vector", lambda e, h=h, st4=st4, XB=XB: e.tensor_tensor(
                                    sev[st4][:, h * D:(h + 1) * D],
                                    psO[:, st4 * 512:st4 * 512 + D],
                                    XB[:, h * D:(h + 1) * D], ALU.add),
                                    reads=[("psO", st4), f"xnb{gpar * TPG + st4}"],
                                    writes=[f"sev{st4}"], tag=f"sv{idx}_{h}_{st4}")
                                if h == QH - 1:
                                    p.dma("scalar", lambda e, bb=bb, g0=g0, st4=st4: e.dma_start(
                                        out=s_rows[bb * T + g0 + st4 * 128:
                                                   bb * T + g0 + (st4 + 1) * 128, :],
                                        in_=sev[st4][:]),
                                        reads=[f"sev{st4}"], writes=["s_rows"],
                                        channel=f"ss{st4 % 2}", tag=f"ssv{idx}_{st4}")


            def emit_tile_p2(bb, tt, estep):
                    idx = bb * NT + tt
                    gpar = (idx // TPG) % 2
                    tp_ = estep % 2
                    st6 = st6_[tp_]
                    mv = mv_[tp_]
                    sc1 = sc1_[tp_]
                    sd2 = sd2_[tp_]
                    mu2 = mu2_[tp_]
                    r2c = r2c_[tp_]
                    lgt = lgt_[tp_]
                    eqt = eqt_[tp_]
                    cnd = cnd_[tp_]
                    msk = msk_[tp_]
                    exq = exq_[tp_]
                    mm1 = mm1_[tp_]
                    mm2t = mm2t_[tp_]
                    dent = dent_[tp_]
                    ipack = ipack_[tp_]
                    bpar = bb
                    p.op("vector", lambda e: e.tensor_reduce(mm1[:], lgt[:], AX.X, ALU.max),
                         reads=[f"lgt{tp_}"], writes=[f"mm1{tp_}"], tag=f"m1{idx}")
                    p.op("vector", lambda e: e.tensor_scalar(eqt[:], lgt[:], mm1[:], None, ALU.is_equal),
                         reads=[f"lgt{tp_}", f"mm1{tp_}"], writes=[f"eqt{tp_}"], tag=f"eqa{idx}")
                    p.op("vector", lambda e: e.tensor_tensor(cnd[:], eqt[:], iotaRB[:], ALU.mult),
                         reads=[f"eqt{tp_}", "iotaRB"], writes=[f"cnd{tp_}"], tag=f"c1{idx}")
                    p.op("vector", lambda e: e.tensor_scalar(cnd[:], cnd[:], BIGC, None, ALU.add),
                         reads=[f"cnd{tp_}"], writes=[f"cnd{tp_}"], tag=f"c1b{idx}")
                    p.op("vector", lambda e: e.tensor_reduce(ipack[:, 0:1], cnd[:], AX.X, ALU.min),
                         reads=[f"cnd{tp_}"], writes=[f"ipack{tp_}"], tag=f"i1{idx}")
                    p.op("vector", lambda e: e.tensor_scalar(exq[:], lgt[:], mm1[:], None, ALU.subtract),
                         reads=[f"lgt{tp_}", f"mm1{tp_}"], writes=[f"exq{tp_}"], tag=f"exs{idx}")
                    p.op("scalar", lambda e: e.activation(exq[:], exq[:], AF.Exp),
                         reads=[f"exq{tp_}"], writes=[f"exq{tp_}"], tag=f"exe{idx}")
                    p.op("vector", lambda e: e.tensor_reduce(dent[:], exq[:], AX.X, ALU.add),
                         reads=[f"exq{tp_}"], writes=[f"dent{tp_}"], tag=f"dn{idx}")
                    p.op("vector", lambda e: e.reciprocal(pk0sb[bpar][:, tt:tt + 1], dent[:]),
                         reads=[f"dent{tp_}"], writes=[f"pk0sb{bpar}"], tag=f"p1{idx}")
                    p.op("vector", lambda e: e.tensor_scalar(eqt[:], iotaR[:], ipack[:, 0:1], None, ALU.is_equal),
                         reads=["iotaR", f"ipack{tp_}"], writes=[f"eqt{tp_}"], tag=f"eqb{idx}")
                    p.op("vector", lambda e: e.scalar_tensor_tensor(
                        msk[:], eqt[:], -BIGC, lgt[:], ALU.mult, ALU.add),
                        reads=[f"eqt{tp_}", f"lgt{tp_}"], writes=[f"msk{tp_}"], tag=f"mka{idx}")
                    p.op("vector", lambda e: e.tensor_reduce(mm2t[:], msk[:], AX.X, ALU.max),
                         reads=[f"msk{tp_}"], writes=[f"mm2t{tp_}"], tag=f"m2{idx}")
                    p.op("vector", lambda e: e.tensor_scalar(eqt[:], msk[:], mm2t[:], None, ALU.is_equal),
                         reads=[f"msk{tp_}", f"mm2t{tp_}"], writes=[f"eqt{tp_}"], tag=f"eqc{idx}")
                    p.op("vector", lambda e: e.tensor_tensor(cnd[:], eqt[:], iotaRB[:], ALU.mult),
                         reads=[f"eqt{tp_}", "iotaRB"], writes=[f"cnd{tp_}"], tag=f"c2{idx}")
                    p.op("vector", lambda e: e.tensor_scalar(cnd[:], cnd[:], BIGC, None, ALU.add),
                         reads=[f"cnd{tp_}"], writes=[f"cnd{tp_}"], tag=f"c2b{idx}")
                    p.op("vector", lambda e: e.tensor_reduce(ipack[:, 1:2], cnd[:], AX.X, ALU.min),
                         reads=[f"cnd{tp_}"], writes=[f"ipack{tp_}"], tag=f"i2{idx}")
                    p.op("vector", lambda e: e.tensor_scalar(mm2t[:], mm2t[:], mm1[:], None, ALU.subtract),
                         reads=[f"mm2t{tp_}", f"mm1{tp_}"], writes=[f"mm2t{tp_}"], tag=f"d2{idx}")
                    p.op("scalar", lambda e: e.activation(mm2t[:], mm2t[:], AF.Exp),
                         reads=[f"mm2t{tp_}"], writes=[f"mm2t{tp_}"], tag=f"d2e{idx}")
                    p.op("vector", lambda e: e.tensor_tensor(
                        pk1sb[bpar][:, tt:tt + 1], mm2t[:], pk0sb[bpar][:, tt:tt + 1], ALU.mult),
                        reads=[f"mm2t{tp_}", f"pk0sb{bpar}"], writes=[f"pk1sb{bpar}"], tag=f"p2{idx}")
                    p.dma("scalar", lambda e, bb=bb, tt=tt: e.dma_start(
                        out=ef_h[bb, tt * 128:(tt + 1) * 128, :], in_=ipack[:]),
                        reads=[f"ipack{tp_}"], writes=["ef_h"], channel="pks", tag=f"se{idx}")


            def emit_routing(bb):
                p.dma("gpsimd", lambda e, bb=bb: e.dma_start(
                    out=efr[:], in_=ef_h[bb].rearrange("t k -> (t k)").unsqueeze(0)),
                    reads=["ef_h", "efr"], writes=["efr"], channel="rld", tag=f"lef{bb}")
                for c0 in range(0, S, 512):
                    p.op("tensor", lambda e, c0=c0: e.matmul(
                        psSC[0][0:8, :], ones18[:], efr[:, c0:c0 + 512],
                        start=True, stop=True),
                        reads=["efr", "ones18"], writes=["psSC0"], tag=f"bce{bb}_{c0}")
                    p.op("vector", lambda e, c0=c0: e.tensor_scalar(
                        gA[:, c0:c0 + 512], psSC[0][0:8, :], iotaE8[:], None, ALU.is_equal),
                        reads=["psSC0", "iotaE8"], writes=["gA"], tag=f"oh{bb}_{c0}")
                p.op("vector", lambda e: e.tensor_tensor_scan(
                    gB[:], gA[:], onesS[:], 0.0, ALU.add, ALU.mult),
                    reads=["gA", "onesS"], writes=["gB"], tag=f"scan{bb}")
                p.op("vector", lambda e: e.scalar_tensor_tensor(
                    gC[:], gB[:], float(CAP), gA[:], ALU.is_le, ALU.mult),
                    reads=["gB", "gA"], writes=["gC"], tag=f"keep{bb}")
                p.op("vector", lambda e: e.tensor_scalar(
                    gB[:], gB[:], -1.0, float(CAP - 1), ALU.add, ALU.min),
                    reads=["gB"], writes=["gB"], tag=f"posc{bb}")
                p.op("vector", lambda e, bb=bb: e.tensor_scalar(
                    gB[:], gB[:], colER[:], float(bb * CAP - DUMP), ALU.add, ALU.add),
                    reads=["gB", "colER"], writes=["gB"], tag=f"poff{bb}")
                p.op("vector", lambda e: e.tensor_tensor(gC[:], gC[:], gB[:], ALU.mult),
                     reads=["gC", "gB"], writes=["gC"], tag=f"idxg{bb}")
                for c0 in range(0, S, 512):
                    p.op("tensor", lambda e, c0=c0: e.matmul(
                        psSC[1][0:1, :], ones81[:], gC[:, c0:c0 + 512],
                        start=True, stop=True),
                        reads=["gC", "ones81"], writes=["psSC1"], tag=f"sid{bb}_{c0}")
                    p.op("vector", lambda e, c0=c0: e.tensor_scalar(
                        efr[:, c0:c0 + 512], psSC[1][0:1, :], float(DUMP), None, ALU.add),
                        reads=["psSC1"], writes=["efr"], tag=f"idq{bb}_{c0}")
                p.op("vector", lambda e: e.tensor_copy(idxi[:], efr[:]),
                     reads=["efr"], writes=["idxi"], tag=f"cvt{bb}")
                p.dma("gpsimd", lambda e, bb=bb: e.dma_start(out=idxq_h[bb, :], in_=idxi[0:1, :]),
                      reads=["idxi"], writes=["idxq_h"], channel="rst", tag=f"sxq{bb}")
                for k in range(2):
                    idxw = idxw_[k]
                    p.dma("gpsimd", lambda e, bb=bb, k=k, idxw=idxw: e.dma_start(
                        out=idxw[0:16, :],
                        in_=idxq_h[bb].rearrange("(c p k) -> k p c", p=16, k=2)[k],
                        allow_slow_non_contiguous=True),
                        reads=["idxq_h", f"idxw{k}"], writes=[f"idxw{k}"], channel=f"iw{k}",
                        tag=f"wr{bb}{k}")
                    for rep in range(1, 8):
                        p.dma("gpsimd", lambda e, rep=rep, idxw=idxw: e.dma_start(
                            out=idxw[16 * rep:16 * rep + 16, :], in_=idxw[0:16, :]),
                            reads=[f"idxw{k}"], writes=[f"idxw{k}"], channel=f"iwr{k}",
                            tag=f"wrr{bb}{k}{rep}")
                    p.dma("gpsimd", lambda e, bb=bb, k=k: e.dma_scatter_add(
                        out_ap=glist_h[:, 0:1],
                        in_ap=valw[:, bb * NT:(bb + 1) * NT].rearrange("p (a c) -> p c a", a=1),
                        idxs_ap=idxw_[k][:], num_idxs=T, num_idxs_reg=T, elem_size=1, elem_step=64),
                        reads=["valw", f"idxw{k}", "glist"], writes=["glist"],
                        channel="sctg", tag=f"gsc{bb}{k}")
                    pksrc = (pk0sb, pk1sb)[k][bb]
                    pkkey = f"pk{k}sb{bb}"
                    p.dma("gpsimd", lambda e, pksrc=pksrc, k=k: e.dma_scatter_add(
                        out_ap=gate_h[:, 0:1],
                        in_ap=pksrc[:].rearrange("p (c a) -> p c a", a=1),
                        idxs_ap=idxw_[k][:], num_idxs=T, num_idxs_reg=T, elem_size=1, elem_step=64),
                        reads=[pkkey, f"idxw{k}", "gate"], writes=["gate", pkkey],
                        channel="sctp", tag=f"psc{bb}{k}")
                    p.dma("gpsimd", lambda e, bb=bb, k=k: e.dma_start(
                        out=ydix[bb][k][0:16, :],
                        in_=idxq_h[bb].rearrange("(c p k) -> k p c", p=16, k=2)[k],
                        allow_slow_non_contiguous=True),
                        reads=["idxq_h", f"ydix{bb}_{k}"], writes=[f"ydix{bb}_{k}"],
                        channel=f"iw{k}", tag=f"ywA{bb}{k}")
                    for rep in range(1, 8):
                        p.dma("gpsimd", lambda e, bb=bb, k=k, rep=rep: e.dma_start(
                            out=ydix[bb][k][16 * rep:16 * rep + 16, :], in_=ydix[bb][k][0:16, :]),
                            reads=[f"ydix{bb}_{k}"], writes=[f"ydix{bb}_{k}"],
                            channel=f"iwr{k}", tag=f"ywAr{bb}{k}{rep}")

            GCHa = 640 if R % 640 == 0 else min(R, 512)
            NGCHa = R // GCHa

            def prep_lists(e_):
                pe = e_ % 2
                for gq in range(NGCHa):
                    r0 = gq * GCHa
                    p.dma("sync", lambda e, e_=e_, r0=r0: e.dma_start(
                        out=glf[:, 0:GCHa // 16],
                        in_=glist_h[e_ * R + r0:e_ * R + r0 + GCHa, 0:1].rearrange(
                            "(c p) o -> p (c o)", p=16),
                        allow_slow_non_contiguous=True),
                        reads=["glist", "glf"], writes=["glf"], channel=f"glr{e_ % 2}",
                        tag=f"glf{e_}_{gq}")
                    p.op("vector", lambda e, pe=pe, gq=gq: e.tensor_copy(
                        glx[pe][0:16, gq * (GCHa // 16):(gq + 1) * (GCHa // 16)],
                        glf[:, 0:GCHa // 16]),
                        reads=["glf"], writes=[f"glx{pe}"], tag=f"glc{e_}_{gq}")
                for rep in range(1, 8):
                    p.dma("scalar", lambda e, pe=pe, rep=rep: e.dma_start(
                        out=glx[pe][16 * rep:16 * rep + 16, :], in_=glx[pe][0:16, :]),
                        reads=[f"glx{pe}"], writes=[f"glx{pe}"], channel=f"glr2{pe}",
                        tag=f"glrp{e_}{rep}")
                p.dma("sync", lambda e, e_=e_, pe=pe: e.dma_start(
                    out=gatec[pe][:],
                    in_=gate_h[e_ * R:(e_ + 1) * R, 0:1].rearrange("(c p) o -> p (c o)", p=128),
                    allow_slow_non_contiguous=True),
                    reads=["gate", f"gatec{pe}"], writes=[f"gatec{pe}"], channel=f"glr{pe}",
                    tag=f"gld{e_}")

            for step, (bb, tt) in enumerate(ORDER):
                emit_tile_p1(bb, tt, step)
                if step > 0:
                    pb, ptt = ORDER[step - 1]
                    emit_tile_p2(pb, ptt, step - 1)
                if tt == 1 and bb > 0:
                    emit_routing(bb - 1)
            lb, ltt = ORDER[-1]
            emit_tile_p2(lb, ltt, len(ORDER) - 1)
            emit_routing(B - 1)
            prep_lists(0)

            p.barrier()

        ctxB = contextlib.ExitStack()
        with ctxB:
            sbb, psb = mk(ctxB)
            GCH = 640 if R % 640 == 0 else min(R, 512)
            NGCH = R // GCH
            SG = 320 if GCH % 320 == 0 else GCH
            assert R % GCH == 0 and GCH < 1024 and GCH % SG == 0 and SG <= 512
            xnTe = [sbb(f"xnTe{i}", [128, NGCH, FB, GCH], BF16) for i in range(2)]
            w1g = [sbb(f"w1g{i}", [128, FB, 128], BF16) for i in range(4)]
            w2sb = sbb("w2sb", [128, HB, C], BF16)
            hT = sbb("hT", [128, HB, R], BF16)
            ysb = [sbb(f"ysb{i}", [128, C], BF16) for i in range(2)]
            b2r = [sbb(f"b2r{i}", [1, C], BF16) for i in range(2)]
            psH = [psb(f"psH{i}", [128, 512]) for i in range(2)]
            psY = [psb(f"psY{i}", [128, C]) for i in range(2)]

            def prep_expert(e_):
                pe = e_ % 2
                if e_ > 0:
                    prep_lists(e_)
                for gq in range(NGCH):
                    r0 = gq * GCH
                    p.dma("gpsimd", lambda e, pe=pe, gq=gq: e.dma_gather(
                        out_ap=xnTe[pe][:, gq, :, :], in_ap=xn_rows,
                        idxs_ap=glx[pe][:, gq * (GCH // 16):(gq + 1) * (GCH // 16)],
                        num_idxs=GCH, num_idxs_reg=GCH, elem_size=C, transpose=True),
                        reads=["xn_rows", f"glx{pe}", f"xnTe{pe}"], writes=[f"xnTe{pe}"],
                        channel=f"gth{pe}", tag=f"gth{e_}_{gq}")

            def load_w2_chunk(e_, ck):
                hb0 = ck * (HB // 8)
                hb1 = (ck + 1) * (HB // 8)
                p.dma("gpsimd", lambda e, e_=e_, hb0=hb0, hb1=hb1: e.dma_start(
                    out=w2sb[:, hb0:hb1, :],
                    in_=w2_h[e_].rearrange("(hb p) c -> p hb c", p=128)[:, hb0:hb1, :]),
                    reads=["w2sb"], writes=["w2sb"], channel="w2l", tag=f"w2{e_}_{ck}")

            def load_w2(e_):
                load_w2_chunk(e_, 0)
                p.dma("gpsimd", lambda e, e_=e_: e.dma_start(
                    out=b2r[e_ % 2][:], in_=b2_h[e_].unsqueeze(0)),
                    reads=[f"b2r{e_ % 2}"], writes=[f"b2r{e_ % 2}"], channel="bl",
                    tag=f"b2{e_}")

            def load_w1g(e_, hb, buf):
                p.dma("gpsimd", lambda e, e_=e_, hb=hb, buf=buf: e.dma_start(
                    out=w1g[buf][:],
                    in_=w1_h[e_].rearrange("(fb p) h -> p fb h", p=128)[:, :, hb * 128:(hb + 1) * 128]),
                    reads=[f"w1g{buf}"], writes=[f"w1g{buf}"], channel=f"w1l{buf}",
                    tag=f"w1{e_}_{hb}")

            p.op("vector", lambda e: e.memset(ysb[0][:], 0.0), writes=["ysb0"], tag="zy")
            p.dma("scalar", lambda e: e.dma_start(out=y_store[ER:ER + 1, :], in_=ysb[0][0:1, :]),
                  reads=["ysb0"], writes=["y_store"], channel="ys0", tag="zyr")

            prep_expert(0)
            for j in range(3):
                if j < E * HB:
                    load_w1g(j // HB, j % HB, j % 4)
            for e_ in range(E):
                pe = e_ % 2
                load_w2(e_)
                if e_ + 1 < E:
                    prep_expert(e_ + 1)
                for hb in range(HB):
                    nh = e_ * HB + hb + 3
                    if nh < E * HB:
                        load_w1g(nh // HB, nh % HB, nh % 4)
                    step = max(1, HB // 8)
                    if (hb + 1) % step == 0 and (hb + 1) // step < 8:
                        load_w2_chunk(e_, (hb + 1) // step)
                    for rg in range(R // SG):
                        gq, so = divmod(rg * SG, GCH)
                        p_h = psH[rg % 2]
                        for fb in range(FB):
                            p.op("tensor", lambda e, hb=hb, rg=rg, fb=fb, pe=pe, gq=gq, so=so, p_h=p_h: e.matmul(
                                p_h[:, 0:SG], w1g[(e_ * HB + hb) % 4][:, fb, :],
                                xnTe[pe][:, gq, fb, so:so + SG],
                                start=(fb == 0), stop=(fb == FB - 1)),
                                reads=[f"w1g{(e_ * HB + hb) % 4}", f"xnTe{pe}"], writes=[f"psH{rg % 2}"],
                                tag=f"h{e_}_{hb}_{rg}_{fb}")
                        p.op("scalar", lambda e, hb=hb, rg=rg, e_=e_, p_h=p_h: e.activation(
                            hT[:, hb, rg * SG:(rg + 1) * SG], p_h[:, 0:SG], AF.Relu,
                            bias=b1all[:, e_ * HB + hb:e_ * HB + hb + 1], scale=1.0),
                            reads=[f"psH{rg % 2}", "b1all"], writes=["hT"],
                            tag=f"he{e_}_{hb}_{rg}")
                NH = C // 512
                for tc in range(RT):
                    for hf in range(NH):
                        for hb in range(HB):
                            p.op("tensor", lambda e, tc=tc, hb=hb, hf=hf: e.matmul(
                                psY[tc % 2][:, hf * 512:(hf + 1) * 512],
                                hT[:, hb, tc * 128:(tc + 1) * 128],
                                w2sb[:, hb, hf * 512:(hf + 1) * 512],
                                start=(hb == 0), stop=False, skip_group_check=True),
                                reads=["hT", "w2sb"], writes=[f"psY{tc % 2}"],
                                tag=f"ya{e_}_{tc}_{hf}_{hb}")
                        p.op("tensor", lambda e, tc=tc, pe=pe, hf=hf: e.matmul(
                            psY[tc % 2][:, hf * 512:(hf + 1) * 512], onesK1b[:],
                            b2r[pe][0:1, hf * 512:(hf + 1) * 512],
                            start=False, stop=True, skip_group_check=True),
                            reads=["onesK1b", f"b2r{pe}"], writes=[f"psY{tc % 2}"],
                            tag=f"yb{e_}_{tc}_{hf}")
                    p.op("scalar", lambda e, tc=tc, pe=pe: e.activation(
                        ysb[tc % 2][:], psY[tc % 2][:], AF.Identity,
                        scale=gatec[pe][:, tc:tc + 1]),
                        reads=[f"psY{tc % 2}", f"gatec{pe}"], writes=[f"ysb{tc % 2}"],
                        tag=f"ye{e_}_{tc}")
                    p.dma("scalar", lambda e, e_=e_, tc=tc: e.dma_start(
                        out=y_store[e_ * R + tc * 128: e_ * R + (tc + 1) * 128, :],
                        in_=ysb[tc % 2][:]),
                        reads=[f"ysb{tc % 2}"], writes=["y_store"], channel=f"ys{tc % 2}",
                        tag=f"yst{e_}_{tc}")

            p.barrier()

        ctxC = contextlib.ExitStack()
        with ctxC:
            sbc, psc = mk(ctxC)
            gth = [[sbc(f"gthc{k}_{i}", [128, NT * C], BF16) for i in range(2)]
                   for k in range(2)]
            xtc = [sbc(f"xtc{i}", [128, C], F32) for i in range(2)]
            stc = [sbc(f"stc{i}", [128, C], BF16) for i in range(2)]
            gpre = [sbc(f"gpre{i}", [128, C], F32) for i in range(2)]
            otc = [sbc(f"otc{i}", [128, C], F32) for i in range(2)]

            for bb in range(B):
                bp = bb % 2
                YCH = min(T, 1024)
                NYCH = T // YCH
                for k in range(2):
                    for yq in range(NYCH):
                        p.dma("gpsimd", lambda e, bb=bb, k=k, bp=bp, yq=yq: e.dma_gather(
                            out_ap=gth[k][bp][:].rearrange("p (n c) -> p n c", n=NT)[
                                :, yq * (YCH // 128):(yq + 1) * (YCH // 128), :],
                            in_ap=y_store,
                            idxs_ap=ydix[bb][k][:, yq * (YCH // 16):(yq + 1) * (YCH // 16)],
                            num_idxs=YCH, num_idxs_reg=YCH, elem_size=C, transpose=False),
                            reads=["y_store", f"ydix{bb}_{k}", f"gthc{k}_{bp}"],
                            writes=[f"gthc{k}_{bp}"], channel=f"gbk{k}", tag=f"gb{bb}{k}_{yq}")
                for tc in range(NT):
                    pc = tc % 2
                    p.dma("sync", lambda e, bb=bb, tc=tc, pc=pc: e.dma_start(
                        out=xtc[pc][:], in_=x_in[bb, tc * 128:(tc + 1) * 128, :]),
                        reads=[f"xtc{pc}"], writes=[f"xtc{pc}"], channel=f"xc{pc}",
                        tag=f"xc{bb}_{tc}")
                    p.dma("sync", lambda e, bb=bb, tc=tc, pc=pc: e.dma_start(
                        out=stc[pc][:], in_=s_rows[bb * T + tc * 128: bb * T + (tc + 1) * 128, :]),
                        reads=["s_rows", f"stc{pc}"], writes=[f"stc{pc}"], channel=f"scl{pc}",
                        tag=f"scl{bb}_{tc}")
                    p.op("vector", lambda e, tc=tc, pc=pc, bp=bp: e.tensor_tensor(
                        gpre[pc][:], gth[0][bp][:, tc * C:(tc + 1) * C],
                        gth[1][bp][:, tc * C:(tc + 1) * C], ALU.add),
                        reads=[f"gthc0_{bp}", f"gthc1_{bp}"], writes=[f"gpre{pc}"],
                        tag=f"gaa{bb}_{tc}")
                    p.op("scalar", lambda e, pc=pc: e.activation(gpre[pc][:], gpre[pc][:], AF.Silu),
                         reads=[f"gpre{pc}"], writes=[f"gpre{pc}"], tag=f"si{bb}_{tc}")
                    p.op("vector", lambda e, pc=pc: e.tensor_tensor(
                        gpre[pc][:], stc[pc][:], gpre[pc][:], ALU.mult),
                        reads=[f"stc{pc}", f"gpre{pc}"], writes=[f"gpre{pc}"], tag=f"sg{bb}_{tc}")
                    p.op("vector", lambda e, pc=pc: e.tensor_tensor(
                        otc[pc][:], xtc[pc][:], gpre[pc][:], ALU.add),
                        reads=[f"xtc{pc}", f"gpre{pc}"], writes=[f"otc{pc}"], tag=f"ob{bb}_{tc}")
                    p.dma("sync", lambda e, bb=bb, tc=tc, pc=pc: e.dma_start(
                        out=out_h[bb, tc * 128:(tc + 1) * 128, :], in_=otc[pc][:]),
                        reads=[f"otc{pc}"], writes=["outw"], channel=f"os{pc}",
                        tag=f"os{bb}_{tc}")

            p.op("sync", lambda e: e.nop(), reads=["outw"], tag="fin")

        with nc.Block() as blk:
            p.build(blk, ctx)
    return nc


def kernel(L0_attn_residual, L0_attn_w_fc, L0_attn_w_proj, L0_attn_ln_g,
           L0_attn_ln_b, L0_ffn_Wr, L0_ffn_W1, L0_ffn_b1, L0_ffn_W2, L0_ffn_b2):
    cfg = default_cfg()
    n_cores = 8
    B = cfg["B"]
    x = np.ascontiguousarray(np.asarray(L0_attn_residual, dtype=np.float32))
    shared = {
        "w_fc": np.ascontiguousarray(np.asarray(L0_attn_w_fc, dtype=np.float32)),
        "w_proj": np.ascontiguousarray(np.asarray(L0_attn_w_proj, dtype=np.float32)),
        "ln_g": np.ascontiguousarray(np.asarray(L0_attn_ln_g, dtype=np.float32)),
        "ln_b": np.ascontiguousarray(np.asarray(L0_attn_ln_b, dtype=np.float32)),
        "wr": np.ascontiguousarray(np.asarray(L0_ffn_Wr, dtype=np.float32)),
        "w1": np.ascontiguousarray(np.asarray(L0_ffn_W1, dtype=np.float32)),
        "b1": np.ascontiguousarray(np.asarray(L0_ffn_b1, dtype=np.float32)),
        "w2": np.ascontiguousarray(np.asarray(L0_ffn_W2, dtype=np.float32)),
        "b2": np.ascontiguousarray(np.asarray(L0_ffn_b2, dtype=np.float32)),
    }
    nc = build_nc(cfg)
    if not nc.is_finalized():
        nc.finalize()
    in_maps = []
    for c in range(n_cores):
        m = dict(shared)
        m["x"] = np.ascontiguousarray(x[c * B:(c + 1) * B])
        in_maps.append(m)
    res = run_bass_kernel_spmd(nc, in_maps, core_ids=list(range(n_cores)))
    out = np.concatenate([np.asarray(r["out"]) for r in res.results], axis=0)
    return out.astype(np.float32)


# version 21
# speedup vs baseline: 1.1647x; 1.0684x over previous
import contextlib
from collections import defaultdict

import numpy as np

import concourse.bass as bass
import concourse.bacc as bacc
import concourse.mybir as mybir
from concourse.bass_utils import run_bass_kernel_spmd

F32 = mybir.dt.float32
BF16 = mybir.dt.bfloat16
I16 = mybir.dt.int16
AF = mybir.ActivationFunctionType
ALU = mybir.AluOpType
AX = mybir.AxisListType

ENGINES = ("tensor", "vector", "scalar", "gpsimd", "sync")
BIGC = 65536.0
EPS = 1e-5


class _Op:
    __slots__ = ("engine", "fn", "reads", "writes", "is_dma", "seq",
                 "sem", "inc_val", "post_val", "waits", "clock", "tag")

    def __init__(self, engine, fn, reads, writes, is_dma, tag=""):
        self.engine = engine
        self.fn = fn
        self.reads = reads
        self.writes = writes
        self.is_dma = is_dma
        self.tag = tag
        self.sem = None
        self.inc_val = 0
        self.post_val = 0
        self.waits = {}
        self.clock = {}


class Prog:
    def __init__(self, nc):
        self.nc = nc
        self.ops = []
        self.last_write = {}
        self.readers = defaultdict(list)
        self.eng_clock = {eng: {} for eng in ENGINES}
        self.sem_counts = defaultdict(int)
        self.dma_channels = set()
        self.sem_objs = {}

    def op(self, engine, fn, reads=(), writes=(), tag=""):
        return self._op(engine, fn, reads, writes, False, None, tag)

    def dma(self, engine, fn, reads=(), writes=(), channel=None, tag=""):
        ch = channel or f"dma_{engine}"
        return self._op(engine, fn, reads, writes, True, ch, tag)

    def _op(self, engine, fn, reads, writes, is_dma, channel, tag):
        o = _Op(engine, fn, tuple(reads), tuple(writes), is_dma, tag)
        o.seq = len(self.ops)
        if is_dma:
            o.sem = f"ch_{channel}"
            o.inc_val = 16
            self.dma_channels.add(o.sem)
        else:
            o.sem = f"eng_{engine}"
            o.inc_val = 1

        deps = []
        for k in o.reads:
            w = self.last_write.get(k)
            if w is not None:
                deps.append(w)
        for k in o.writes:
            w = self.last_write.get(k)
            if w is not None:
                deps.append(w)
            deps.extend(self.readers.get(k, ()))

        need = {}
        for d in deps:
            v = self.sem_counts[d.sem] if d.sem in self.dma_channels else d.post_val
            if need.get(d.sem, 0) < v:
                need[d.sem] = v
        if is_dma:
            prev = self.sem_counts[o.sem]
            if prev > 0 and need.get(o.sem, 0) < prev:
                need[o.sem] = prev

        clk = self.eng_clock[engine]
        o.waits = {s: v for s, v in need.items() if clk.get(s, 0) < v}

        self.sem_counts[o.sem] += o.inc_val
        o.post_val = self.sem_counts[o.sem]

        newclk = dict(clk)
        if engine == "tensor" and not is_dma:
            newclk[o.sem] = o.post_val
        for s, v in need.items():
            if newclk.get(s, 0) < v:
                newclk[s] = v
        for d in deps:
            for s, v in d.clock.items():
                if s in self.dma_channels:
                    continue
                if newclk.get(s, 0) < v:
                    newclk[s] = v
        self.eng_clock[engine] = newclk
        o.clock = dict(newclk)
        o.clock[o.sem] = o.post_val

        for k in o.writes:
            self.last_write[k] = o
            self.readers[k] = []
        for k in o.reads:
            self.readers[k].append(o)
        self.ops.append(o)
        return o

    def barrier(self):
        totals = dict(self.sem_counts)
        for eng in ENGINES:
            clk = self.eng_clock[eng]
            waits = {s: v for s, v in totals.items() if v > 0 and clk.get(s, 0) < v}
            o = _Op(eng, lambda e: e.nop(), (), (), False, f"bar_{eng}")
            o.seq = len(self.ops)
            o.sem = f"eng_{eng}"
            o.inc_val = 1
            o.waits = waits
            self.sem_counts[o.sem] += 1
            o.post_val = self.sem_counts[o.sem]
            newclk = dict(clk)
            for s, v in totals.items():
                if newclk.get(s, 0) < v:
                    newclk[s] = v
            self.eng_clock[eng] = newclk
            o.clock = dict(newclk)
            o.clock[o.sem] = o.post_val
            self.ops.append(o)

    def build(self, block, exitstack):
        nc = self.nc
        for name in sorted(self.sem_counts):
            self.sem_objs[name] = exitstack.enter_context(nc.semaphore(name))
        needed = defaultdict(int)
        for o in self.ops:
            for s, v in o.waits.items():
                if needed[s] < v:
                    needed[s] = v
        per_engine = defaultdict(list)
        for o in self.ops:
            per_engine[o.engine].append(o)

        def emit_stream(eng_name, eng):
            for o in per_engine[eng_name]:
                for s, v in sorted(o.waits.items()):
                    eng.wait_ge(self.sem_objs[s], v)
                inst = o.fn(eng)
                if o.is_dma or needed[o.sem] >= o.post_val:
                    if inst is None:
                        raise RuntimeError(f"op {o.tag or o.seq} needs inc but returned None")
                    inst.then_inc(self.sem_objs[o.sem], o.inc_val)

        for eng_name in ENGINES:
            if not per_engine[eng_name]:
                continue
            getattr(block, eng_name)(lambda eng, en=eng_name: emit_stream(en, eng))


def default_cfg():
    return dict(B=4, T=2048, C=1024, E=8, CAP=320, H=4096, QH=4)


def build_nc(cfg):
    B, T, C, E, CAP, H, QH = (cfg["B"], cfg["T"], cfg["C"], cfg["E"],
                              cfg["CAP"], cfg["H"], cfg["QH"])
    D = C // QH
    NS = C * C // D
    SC = NS // 128
    FB = C // 128
    HB = H // 128
    DB = D // 128
    NT = T // 128
    S = 2 * T
    R = B * CAP
    assert R % 128 == 0 and T % 16 == 0 and S % 512 == 0
    ER = E * R
    DUMP = ER
    RT = R // 128
    GRP = min(512, T)
    TPG = GRP // 128

    nc = bacc.Bacc("TRN2", target_bir_lowering=False)
    x_in = nc.dram_tensor("x", [B, T, C], F32, kind="ExternalInput").ap()
    w_fc = nc.dram_tensor("w_fc", [C, C], F32, kind="ExternalInput").ap()
    w_proj = nc.dram_tensor("w_proj", [C, C], F32, kind="ExternalInput").ap()
    ln_g = nc.dram_tensor("ln_g", [C], F32, kind="ExternalInput").ap()
    ln_b = nc.dram_tensor("ln_b", [C], F32, kind="ExternalInput").ap()
    wr_h = nc.dram_tensor("wr", [C, E], F32, kind="ExternalInput").ap()
    w1_h = nc.dram_tensor("w1", [E, C, H], F32, kind="ExternalInput").ap()
    b1_h = nc.dram_tensor("b1", [E, H], F32, kind="ExternalInput").ap()
    w2_h = nc.dram_tensor("w2", [E, H, C], F32, kind="ExternalInput").ap()
    b2_h = nc.dram_tensor("b2", [E, C], F32, kind="ExternalInput").ap()
    out_h = nc.dram_tensor("out", [B, T, C], F32, kind="ExternalOutput").ap()

    N = B * T
    xn_rows = nc.dram_tensor("xn_rows", [N, C], BF16, kind="Internal").ap()
    s_rows = nc.dram_tensor("s_rows", [N, C], BF16, kind="Internal").ap()
    y_store = nc.dram_tensor("y_store", [ER + 1, C], BF16, kind="Internal").ap()
    ef_h = nc.dram_tensor("ef_h", [B, T, 2], F32, kind="Internal").ap()
    idxq_h = nc.dram_tensor("idxq_h", [B, S], I16, kind="Internal").ap()
    glist_h = nc.dram_tensor("glist_h", [ER + 1, 64], F32, kind="Internal").ap()
    gate_h = nc.dram_tensor("gate_h", [ER + 1, 64], F32, kind="Internal").ap()

    p = Prog(nc)
    ctx = contextlib.ExitStack()

    with ctx:
        def mk(stack):
            def sb(name, shape, dt):
                return stack.enter_context(nc.sbuf_tensor(name, shape, dt))

            def ps(name, shape):
                return stack.enter_context(nc.psum_tensor(name, shape, F32))
            return sb, ps

        sbg, psg = mk(ctx)
        identF = sbg("identF", [128, 128], F32)
        onesK1b = sbg("onesK1b", [1, 128], BF16)
        Rg = cfg["B"] * cfg["CAP"]
        glf = sbg("glf", [16, Rg // 16], F32)
        glx = [sbg(f"glx{i}", [128, Rg // 16], I16) for i in range(2)]
        gatec = [sbg(f"gatec{i}", [128, Rg // 128], F32) for i in range(2)]
        ydix = [[sbg(f"ydix{b_}_{k_}", [128, cfg["T"] // 16], I16) for k_ in range(2)]
                for b_ in range(cfg["B"])]
        b1all = sbg("b1all", [128, E * HB], F32)

        ctxA = contextlib.ExitStack()
        with ctxA:
            sba, psa = mk(ctxA)
            ones18 = sba("ones18", [1, 8], F32)
            ones81 = sba("ones81", [8, 1], F32)
            iotaR = sba("iotaR", [128, E], F32)
            iotaRB = sba("iotaRB", [128, E], F32)
            iotaE8 = sba("iotaE8", [8, 1], F32)
            colER = sba("colER", [8, 1], F32)
            valw = sba("valw", [128, B * NT], F32)
            onesS = sba("onesS", [8, S], BF16)
            zer = sba("zer", [128, 64], F32)
            epsc = sba("epsc", [128, 1], F32)
            gcol = sba("gcol", [128, FB], F32)
            bcol = sba("bcol", [128, FB], F32)
            wr_sb = sba("wr_sb", [128, FB, E], F32)
            sWr = sba("sWr", [128, E], F32)
            kT = sba("kT", [128, DB, NS], BF16)
            v_sb = sba("v_sb", [128, SC, D], BF16)

            p.op("gpsimd", lambda e: e.iota(identF[:], [[1, 128]], channel_multiplier=-1,
                                            allow_small_or_imprecise_dtypes=True),
                 writes=["identF"], tag="io_id")
            p.op("gpsimd", lambda e: e.iota(iotaR[:], [[1, E]], channel_multiplier=0,
                                            allow_small_or_imprecise_dtypes=True),
                 writes=["iotaR"], tag="io_r")
            p.op("gpsimd", lambda e: e.iota(iotaE8[:], [[0, 1]], channel_multiplier=1,
                                            allow_small_or_imprecise_dtypes=True),
                 writes=["iotaE8"], tag="io_e")
            for bb in range(B):
                for tt in range(NT):
                    j = bb * NT + tt
                    p.op("gpsimd", lambda e, j=j, bb=bb, tt=tt: e.iota(
                        valw[:, j:j + 1], [[0, 1]], base=bb * T + tt * 128,
                        channel_multiplier=1, allow_small_or_imprecise_dtypes=True),
                        writes=["valw"], tag=f"io_v{j}")

            p.op("vector", lambda e: e.tensor_scalar(identF[:], identF[:], 0.0, None, ALU.is_equal),
                 reads=["identF"], writes=["identF"], tag="mkident")
            p.op("vector", lambda e: e.tensor_scalar(iotaRB[:], iotaR[:], -BIGC, None, ALU.add),
                 reads=["iotaR"], writes=["iotaRB"])
            p.op("vector", lambda e: e.tensor_scalar(colER[:], iotaE8[:], float(R), None, ALU.mult),
                 reads=["iotaE8"], writes=["colER"])
            p.op("vector", lambda e: e.memset(ones18[:], 1.0), writes=["ones18"])
            p.op("vector", lambda e: e.memset(ones81[:], 1.0), writes=["ones81"])
            p.op("vector", lambda e: e.memset(onesK1b[:], 1.0), writes=["onesK1b"])
            p.op("vector", lambda e: e.memset(onesS[:], 1.0), writes=["onesS"])
            p.op("vector", lambda e: e.memset(zer[:], 0.0), writes=["zer"])
            p.op("vector", lambda e: e.memset(epsc[:], EPS), writes=["epsc"])

            nzb = (ER + 1 + 127) // 128
            for i in range(nzb):
                r0, r1 = i * 128, min((i + 1) * 128, ER + 1)
                p.dma("sync", lambda e, r0=r0, r1=r1: e.dma_start(
                    out=glist_h[r0:r1, :], in_=zer[0:r1 - r0, :]),
                    reads=["zer"], writes=["glist"], channel="zg", tag=f"zg{i}")
                p.dma("sync", lambda e, r0=r0, r1=r1: e.dma_start(
                    out=gate_h[r0:r1, :], in_=zer[0:r1 - r0, :]),
                    reads=["zer"], writes=["gate"], channel="zg", tag=f"zp{i}")

            p.dma("sync", lambda e: e.dma_start(
                out=gcol[:], in_=ln_g.rearrange("(f p) -> p f", p=128), allow_slow_non_contiguous=True),
                writes=["gcol"], channel="cst", tag="lg")
            p.dma("sync", lambda e: e.dma_start(
                out=bcol[:], in_=ln_b.rearrange("(f p) -> p f", p=128), allow_slow_non_contiguous=True),
                writes=["bcol"], channel="cst", tag="lb")
            p.dma("sync", lambda e: e.dma_start(
                out=wr_sb[:], in_=wr_h.rearrange("(f p) e -> p f e", p=128), allow_slow_non_contiguous=True),
                writes=["wr_sb"], channel="cst", tag="lwr")
            p.dma("sync", lambda e: e.dma_start(
                out=b1all[:], in_=b1_h.rearrange("e (h p) -> p (e h)", p=128), allow_slow_non_contiguous=True),
                writes=["b1all"], channel="cst", tag="lb1")

            with contextlib.ExitStack() as c0:
                sb0, ps0 = mk(c0)
                stg = sb0("stg", [128, FB, C], F32)
                tbf = sb0("tbf", [128, FB, C], BF16)
                onesPE = sb0("onesPE", [128, 128], F32)
                psT0 = ps0("psT0", [128, 1024])
                psW = ps0("psW", [128, 512])
                p.op("vector", lambda e: e.memset(onesPE[:], 1.0), writes=["onesPE"])
                p.op("vector", lambda e: e.memset(v_sb[:], 0.0), writes=["v_sb"])
                p.op("vector", lambda e: e.memset(kT[:], 0.0), writes=["kT"])

                for wname, hbuf in (("fc", w_fc), ("pj", w_proj)):
                    p.dma("sync", lambda e, hbuf=hbuf: e.dma_start(
                        out=stg[:], in_=hbuf.rearrange("(a p) c -> p a c", p=128)),
                        reads=["stg"], writes=["stg"], channel="cst", tag=f"lw{wname}")
                    for rb in range(FB):
                        for cb in range(FB):
                            par = (rb * FB + cb) % 2
                            pk_ = ("psT0", par)
                            p.op("tensor", lambda e, rb=rb, cb=cb, par=par: e.transpose(
                                psT0[:, par * 512:par * 512 + 128],
                                stg[:, rb, cb * 128:(cb + 1) * 128], identF[:]),
                                reads=["stg", "identF"], writes=[pk_], tag=f"T{wname}{rb}{cb}")
                            p.op("vector", lambda e, rb=rb, cb=cb, par=par: e.tensor_copy(
                                tbf[:, cb, rb * 128:(rb + 1) * 128],
                                psT0[:, par * 512:par * 512 + 128]),
                                reads=[pk_], writes=["tbf"], tag=f"E{wname}{rb}{cb}")
                    if wname == "fc":
                        for db in range(DB):
                            for sr in range(4):
                                cb = (sr * D + db * 128) // 128
                                p.op("vector", lambda e, db=db, sr=sr, cb=cb: e.tensor_copy(
                                    kT[:, db, sr::4], tbf[:, cb, 0:NS // 4]),
                                    reads=["tbf"], writes=["kT"], tag=f"k{db}{sr}")
                        for fb in range(FB):
                            p.op("tensor", lambda e, fb=fb: e.matmul(
                                psW[:, 0:E], onesPE[:], wr_sb[:, fb, :],
                                start=(fb == 0), stop=(fb == FB - 1)),
                                reads=["onesPE", "wr_sb"], writes=["psW"], tag=f"sw{fb}")
                        p.op("vector", lambda e: e.tensor_copy(sWr[:], psW[:, 0:E]),
                             reads=["psW"], writes=["sWr"])
                    else:
                        for q in range(4):
                            for w in range(4):
                                p.dma("scalar", lambda e, q=q, w=w: e.dma_start(
                                    out=v_sb[q:q + 125:4, :, :].rearrange(
                                        "p (u w2) d -> p u w2 d", w2=4)[:, :, w, :],
                                    in_=tbf[32 * w:32 * w + 32, :, q * D:(q + 1) * D]),
                                    reads=["tbf"], writes=["v_sb"], channel="vsh",
                                    tag=f"v{q}{w}")
                p.barrier()
            sWr_key = "sWr"

            xt = [sba(f"xt{i}", [128, C], F32) for i in range(3)]
            st6_ = [sba(f"st6{i}", [128, 12], F32) for i in range(2)]
            mv_ = [sba(f"mv{i}", [128, 2], F32) for i in range(2)]
            sc1_ = [sba(f"sc1{i}", [128, 4], F32) for i in range(2)]
            sd2_ = [sba(f"sd2{i}", [128, 1], F32) for i in range(2)]
            mu2_ = [sba(f"mu2{i}", [128, 1], F32) for i in range(2)]
            r2c_ = [sba(f"r2c{i}", [128, 2], F32) for i in range(2)]
            xn_f = [sba(f"xn_f{i}", [128, C], F32) for i in range(2)]
            xnb = [sba(f"xnb{i}", [128, C], BF16) for i in range(2 * TPG)]
            ap_f = [sba(f"ap_f{i}", [128, C], F32) for i in range(2)]
            apT = [sba(f"apT{i}", [128, C], F32) for i in range(2)]
            aiT_ = [sba(f"aiT{i}", [128, FB, GRP], BF16) for i in range(2)]
            pT = [sba(f"pT{i}", [128, GRP], BF16) for i in range(2)]
            uT = [sba(f"uT{i}", [128, GRP], BF16) for i in range(2)]
            sev = [sba(f"sev{i}", [128, C], BF16) for i in range(TPG)]
            lgt_ = [sba(f"lgt{i}", [128, E], F32) for i in range(2)]
            eqt_ = [sba(f"eqt{i}", [128, E], F32) for i in range(2)]
            cnd_ = [sba(f"cnd{i}", [128, E], F32) for i in range(2)]
            msk_ = [sba(f"msk{i}", [128, E], F32) for i in range(2)]
            exq_ = [sba(f"exq{i}", [128, E], F32) for i in range(2)]
            mm1_ = [sba(f"mm1{i}", [128, 1], F32) for i in range(2)]
            mm2t_ = [sba(f"mm2t{i}", [128, 1], F32) for i in range(2)]
            dent_ = [sba(f"dent{i}", [128, 1], F32) for i in range(2)]
            ipack_ = [sba(f"ipack{i}", [128, 2], F32) for i in range(2)]
            pk0sb = [sba(f"pk0sb{i}", [128, NT], F32) for i in range(B)]
            pk1sb = [sba(f"pk1sb{i}", [128, NT], F32) for i in range(B)]
            gA = sba("gA", [8, S], F32)
            gB = sba("gB", [8, S], F32)
            gC = sba("gC", [8, S], F32)
            efr = sba("efr", [1, S], F32)
            idxi = sba("idxi", [1, S], I16)
            idxw_ = [sba(f"idxw{i}", [128, T // 16], I16) for i in range(2)]

            psT = [psa("psTs0", [128, 512])]
            psLG = psa("psLG", [128, 512])
            psSC = [psa(f"psSC{i}", [128, 512]) for i in range(2)]
            psO = psa("psO", [128, TPG * 512])

            ORDER = [(bb, tt) for bb in range(B) for tt in range(NT)]

            def xload(step):
                if step >= len(ORDER):
                    return
                bb, tt = ORDER[step]
                p.dma("sync", lambda e, bb=bb, tt=tt, step=step: e.dma_start(
                    out=xt[step % 3][:], in_=x_in[bb, tt * 128:(tt + 1) * 128, :]),
                    reads=[f"xt{step % 3}"], writes=[f"xt{step % 3}"],
                    channel=f"xl{step % 3}", tag=f"xl{step}")

            xload(0)
            xload(1)
            xload(2)

            def emit_tile_p1(bb, tt, estep):
                    idx = bb * NT + tt
                    gpar = (idx // TPG) % 2
                    tp_ = estep % 2
                    st6 = st6_[tp_]
                    mv = mv_[tp_]
                    sc1 = sc1_[tp_]
                    sd2 = sd2_[tp_]
                    mu2 = mu2_[tp_]
                    r2c = r2c_[tp_]
                    lgt = lgt_[tp_]
                    eqt = eqt_[tp_]
                    cnd = cnd_[tp_]
                    msk = msk_[tp_]
                    exq = exq_[tp_]
                    mm1 = mm1_[tp_]
                    mm2t = mm2t_[tp_]
                    dent = dent_[tp_]
                    ipack = ipack_[tp_]
                    bpar = bb
                    aiT = aiT_[gpar]
                    X = xt[estep % 3]
                    xk = f"xt{estep % 3}"
                    XN = xn_f[estep % 2]
                    xnk = f"xn_f{estep % 2}"
                    XNB = xnb[gpar * TPG + tt % TPG]
                    xnbk = f"xnb{gpar * TPG + tt % TPG}"
                    AP_ = ap_f[estep % 2]
                    apk = f"ap_f{estep % 2}"
                    APT = apT[estep % 2]
                    aptk = f"apT{estep % 2}"

                    p.op("vector", lambda e, X=X: e.bn_stats(st6[:, 0:6], X[:, 0:C // 2]),
                         reads=[xk], writes=[f"st6{tp_}"], tag=f"bs1a{idx}")
                    p.op("vector", lambda e, X=X: e.bn_stats(st6[:, 6:12], X[:, C // 2:C]),
                         reads=[xk], writes=[f"st6{tp_}"], tag=f"bs1b{idx}")
                    p.op("vector", lambda e: e.bn_aggr(mv[:], st6[:]),
                         reads=[f"st6{tp_}"], writes=[f"mv{tp_}"], tag=f"ba1{idx}")
                    p.op("scalar", lambda e: e.activation(sc1[:, 0:1], mv[:, 1:2], AF.Sqrt, bias=epsc[:]),
                         reads=[f"mv{tp_}", "epsc"], writes=[f"sc1{tp_}"], tag=f"sq1{idx}")
                    p.op("vector", lambda e: e.reciprocal(sc1[:, 1:2], sc1[:, 0:1]),
                         reads=[f"sc1{tp_}"], writes=[f"sc1{tp_}"], tag=f"rc1{idx}")
                    p.op("vector", lambda e: e.scalar_tensor_tensor(
                        sc1[:, 2:3], mv[:, 0:1], -1.0, sc1[:, 1:2], ALU.mult, ALU.mult),
                        reads=[f"mv{tp_}", f"sc1{tp_}"], writes=[f"sc1{tp_}"], tag=f"b1c{idx}")
                    p.op("scalar", lambda e, X=X, XN=XN: e.activation(
                        XN[:], X[:], AF.Identity, bias=sc1[:, 2:3], scale=sc1[:, 1:2]),
                        reads=[xk, f"sc1{tp_}"], writes=[xnk], tag=f"xn{idx}")
                    p.op("scalar", lambda e, XN=XN, XNB=XNB: e.activation(
                        XNB[:], XN[:], AF.Copy, bias=0.0, scale=1.0),
                        reads=[xnk], writes=[xnbk], tag=f"xnbc{idx}")
                    p.dma("scalar", lambda e, XNB=XNB, bb=bb, tt=tt: e.dma_start(
                        out=xn_rows[bb * T + tt * 128: bb * T + (tt + 1) * 128, :], in_=XNB[:]),
                        reads=[xnbk], writes=["xn_rows"], channel=f"xs{estep % 2}", tag=f"xns{idx}")
                    p.op("vector", lambda e, XN=XN: e.bn_stats(st6[:, 0:6], XN[:, 0:C // 2]),
                         reads=[xnk], writes=[f"st6{tp_}"], tag=f"bs2a{idx}")
                    p.op("vector", lambda e, XN=XN: e.bn_stats(st6[:, 6:12], XN[:, C // 2:C]),
                         reads=[xnk], writes=[f"st6{tp_}"], tag=f"bs2b{idx}")
                    p.op("vector", lambda e: e.bn_aggr(mv[:], st6[:]),
                         reads=[f"st6{tp_}"], writes=[f"mv{tp_}"], tag=f"ba2{idx}")
                    p.op("scalar", lambda e: e.activation(sd2[:], mv[:, 1:2], AF.Sqrt, bias=epsc[:]),
                         reads=[f"mv{tp_}", "epsc"], writes=[f"sd2{tp_}"], tag=f"sq2{idx}")
                    p.op("vector", lambda e: e.tensor_copy(mu2[:], mv[:, 0:1]),
                         reads=[f"mv{tp_}"], writes=[f"mu2{tp_}"], tag=f"mu2{idx}")
                    p.op("vector", lambda e: e.reciprocal(r2c[:, 0:1], sd2[:]),
                         reads=[f"sd2{tp_}"], writes=[f"r2c{tp_}"], tag=f"rc2{idx}")
                    p.op("vector", lambda e: e.scalar_tensor_tensor(
                        r2c[:, 1:2], mu2[:], -1.0, r2c[:, 0:1], ALU.mult, ALU.mult),
                        reads=[f"mu2{tp_}", f"r2c{tp_}"], writes=[f"r2c{tp_}"], tag=f"b2c{idx}")
                    p.op("scalar", lambda e, XN=XN, AP_=AP_: e.activation(
                        AP_[:], XN[:], AF.Identity, bias=r2c[:, 1:2], scale=r2c[:, 0:1]),
                        reads=[xnk, f"r2c{tp_}"], writes=[apk], tag=f"apa{idx}")
                    for fb in range(FB):
                        pk_ = ("psT", 0)
                        p.op("tensor", lambda e, AP_=AP_, fb=fb: e.transpose(
                            psT[0][:, 0:128],
                            AP_[:, fb * 128:(fb + 1) * 128], identF[:]),
                            reads=[apk, "identF"], writes=[pk_], tag=f"tr{idx}_{fb}")
                        p.op("vector", lambda e, fb=fb, APT=APT: e.tensor_copy(
                            APT[:, fb * 128:(fb + 1) * 128],
                            psT[0][:, 0:128]),
                            reads=[pk_], writes=[aptk], tag=f"ae{idx}_{fb}")
                        p.op("scalar", lambda e, fb=fb, tt=tt, APT=APT: e.activation(
                            aiT[:, fb, (tt % TPG) * 128:(tt % TPG + 1) * 128],
                            APT[:, fb * 128:(fb + 1) * 128],
                            AF.Identity, bias=bcol[:, fb:fb + 1], scale=gcol[:, fb:fb + 1]),
                            reads=[aptk, "gcol", "bcol"], writes=[f"aiT{gpar}"], tag=f"ai{idx}_{fb}")
                    for fb in range(FB):
                        p.op("tensor", lambda e, fb=fb, APT=APT: e.matmul(
                            psLG[:, 0:E], APT[:, fb * 128:(fb + 1) * 128], wr_sb[:, fb, :],
                            start=(fb == 0), stop=(fb == FB - 1)),
                            reads=[aptk, "wr_sb"], writes=["psLG"], tag=f"rt{idx}_{fb}")
                    p.op("vector", lambda e: e.tensor_scalar(lgt[:], psLG[:, 0:E], sd2[:], None, ALU.mult),
                         reads=["psLG", f"sd2{tp_}"], writes=[f"lgt{tp_}"], tag=f"lg1{idx}")
                    p.op("vector", lambda e: e.scalar_tensor_tensor(
                        lgt[:], sWr[:], mu2[:], lgt[:], ALU.mult, ALU.add),
                        reads=[sWr_key, f"mu2{tp_}", f"lgt{tp_}"], writes=[f"lgt{tp_}"], tag=f"lg2{idx}")

                    xload(estep + 3)

                    if tt % TPG == TPG - 1:
                        g0 = (tt // TPG) * GRP
                        for h in range(QH):
                            for scj in range(SC + 1):
                                if scj < SC:
                                    for db in range(DB):
                                        p.op("tensor", lambda e, h=h, scj=scj, db=db: e.matmul(
                                            psSC[scj % 2][:, 0:GRP],
                                            kT[:, db, scj * 128:(scj + 1) * 128],
                                            aiT[:, h * DB + db, :],
                                            start=(db == 0), stop=(db == DB - 1),
                                            skip_group_check=True),
                                            reads=["kT", f"aiT{gpar}"], writes=[f"psSC{scj % 2}"],
                                            tag=f"qk{idx}_{h}_{scj}_{db}")
                                if scj > 0:
                                    sc_ = scj - 1
                                    p.op("scalar", lambda e, sc_=sc_: e.activation(
                                        uT[sc_ % 2][:], psSC[sc_ % 2][:, 0:GRP], AF.Relu),
                                        reads=[f"psSC{sc_ % 2}"], writes=[f"uT{sc_ % 2}"],
                                        tag=f"ru_{idx}_{h}_{sc_}")
                                    p.op("vector", lambda e, sc_=sc_: e.tensor_tensor(
                                        pT[sc_ % 2][:], uT[sc_ % 2][:],
                                        psSC[sc_ % 2][:, 0:GRP], ALU.mult),
                                        reads=[f"uT{sc_ % 2}", f"psSC{sc_ % 2}"],
                                        writes=[f"pT{sc_ % 2}"],
                                        tag=f"r2_{idx}_{h}_{sc_}")
                                    for st4 in range(TPG):
                                        p.op("tensor", lambda e, sc_=sc_, st4=st4: e.matmul(
                                            psO[:, st4 * 512:st4 * 512 + D],
                                            pT[sc_ % 2][:, st4 * 128:(st4 + 1) * 128],
                                            v_sb[:, sc_, :],
                                            start=(sc_ == 0), stop=(sc_ == SC - 1),
                                            skip_group_check=True),
                                            reads=[f"pT{sc_ % 2}", "v_sb"],
                                            writes=[("psO", st4)],
                                            tag=f"pv{idx}_{h}_{sc_}_{st4}")
                            for st4 in range(TPG):
                                XB = xnb[gpar * TPG + st4]
                                p.op("vector", lambda e, h=h, st4=st4, XB=XB: e.tensor_tensor(
                                    sev[st4][:, h * D:(h + 1) * D],
                                    psO[:, st4 * 512:st4 * 512 + D],
                                    XB[:, h * D:(h + 1) * D], ALU.add),
                                    reads=[("psO", st4), f"xnb{gpar * TPG + st4}"],
                                    writes=[f"sev{st4}"], tag=f"sv{idx}_{h}_{st4}")
                                if h == QH - 1:
                                    p.dma("scalar", lambda e, bb=bb, g0=g0, st4=st4: e.dma_start(
                                        out=s_rows[bb * T + g0 + st4 * 128:
                                                   bb * T + g0 + (st4 + 1) * 128, :],
                                        in_=sev[st4][:]),
                                        reads=[f"sev{st4}"], writes=["s_rows"],
                                        channel=f"ss{st4 % 2}", tag=f"ssv{idx}_{st4}")


            def emit_tile_p2(bb, tt, estep):
                    idx = bb * NT + tt
                    gpar = (idx // TPG) % 2
                    tp_ = estep % 2
                    st6 = st6_[tp_]
                    mv = mv_[tp_]
                    sc1 = sc1_[tp_]
                    sd2 = sd2_[tp_]
                    mu2 = mu2_[tp_]
                    r2c = r2c_[tp_]
                    lgt = lgt_[tp_]
                    eqt = eqt_[tp_]
                    cnd = cnd_[tp_]
                    msk = msk_[tp_]
                    exq = exq_[tp_]
                    mm1 = mm1_[tp_]
                    mm2t = mm2t_[tp_]
                    dent = dent_[tp_]
                    ipack = ipack_[tp_]
                    bpar = bb
                    p.op("vector", lambda e: e.tensor_reduce(mm1[:], lgt[:], AX.X, ALU.max),
                         reads=[f"lgt{tp_}"], writes=[f"mm1{tp_}"], tag=f"m1{idx}")
                    p.op("vector", lambda e: e.tensor_scalar(eqt[:], lgt[:], mm1[:], None, ALU.is_equal),
                         reads=[f"lgt{tp_}", f"mm1{tp_}"], writes=[f"eqt{tp_}"], tag=f"eqa{idx}")
                    p.op("vector", lambda e: e.tensor_tensor(cnd[:], eqt[:], iotaRB[:], ALU.mult),
                         reads=[f"eqt{tp_}", "iotaRB"], writes=[f"cnd{tp_}"], tag=f"c1{idx}")
                    p.op("vector", lambda e: e.tensor_scalar(cnd[:], cnd[:], BIGC, None, ALU.add),
                         reads=[f"cnd{tp_}"], writes=[f"cnd{tp_}"], tag=f"c1b{idx}")
                    p.op("vector", lambda e: e.tensor_reduce(ipack[:, 0:1], cnd[:], AX.X, ALU.min),
                         reads=[f"cnd{tp_}"], writes=[f"ipack{tp_}"], tag=f"i1{idx}")
                    p.op("vector", lambda e: e.tensor_scalar(exq[:], lgt[:], mm1[:], None, ALU.subtract),
                         reads=[f"lgt{tp_}", f"mm1{tp_}"], writes=[f"exq{tp_}"], tag=f"exs{idx}")
                    p.op("scalar", lambda e: e.activation(exq[:], exq[:], AF.Exp),
                         reads=[f"exq{tp_}"], writes=[f"exq{tp_}"], tag=f"exe{idx}")
                    p.op("vector", lambda e: e.tensor_reduce(dent[:], exq[:], AX.X, ALU.add),
                         reads=[f"exq{tp_}"], writes=[f"dent{tp_}"], tag=f"dn{idx}")
                    p.op("vector", lambda e: e.reciprocal(pk0sb[bpar][:, tt:tt + 1], dent[:]),
                         reads=[f"dent{tp_}"], writes=[f"pk0sb{bpar}"], tag=f"p1{idx}")
                    p.op("vector", lambda e: e.tensor_scalar(eqt[:], iotaR[:], ipack[:, 0:1], None, ALU.is_equal),
                         reads=["iotaR", f"ipack{tp_}"], writes=[f"eqt{tp_}"], tag=f"eqb{idx}")
                    p.op("vector", lambda e: e.scalar_tensor_tensor(
                        msk[:], eqt[:], -BIGC, lgt[:], ALU.mult, ALU.add),
                        reads=[f"eqt{tp_}", f"lgt{tp_}"], writes=[f"msk{tp_}"], tag=f"mka{idx}")
                    p.op("vector", lambda e: e.tensor_reduce(mm2t[:], msk[:], AX.X, ALU.max),
                         reads=[f"msk{tp_}"], writes=[f"mm2t{tp_}"], tag=f"m2{idx}")
                    p.op("vector", lambda e: e.tensor_scalar(eqt[:], msk[:], mm2t[:], None, ALU.is_equal),
                         reads=[f"msk{tp_}", f"mm2t{tp_}"], writes=[f"eqt{tp_}"], tag=f"eqc{idx}")
                    p.op("vector", lambda e: e.tensor_tensor(cnd[:], eqt[:], iotaRB[:], ALU.mult),
                         reads=[f"eqt{tp_}", "iotaRB"], writes=[f"cnd{tp_}"], tag=f"c2{idx}")
                    p.op("vector", lambda e: e.tensor_scalar(cnd[:], cnd[:], BIGC, None, ALU.add),
                         reads=[f"cnd{tp_}"], writes=[f"cnd{tp_}"], tag=f"c2b{idx}")
                    p.op("vector", lambda e: e.tensor_reduce(ipack[:, 1:2], cnd[:], AX.X, ALU.min),
                         reads=[f"cnd{tp_}"], writes=[f"ipack{tp_}"], tag=f"i2{idx}")
                    p.op("vector", lambda e: e.tensor_scalar(mm2t[:], mm2t[:], mm1[:], None, ALU.subtract),
                         reads=[f"mm2t{tp_}", f"mm1{tp_}"], writes=[f"mm2t{tp_}"], tag=f"d2{idx}")
                    p.op("scalar", lambda e: e.activation(mm2t[:], mm2t[:], AF.Exp),
                         reads=[f"mm2t{tp_}"], writes=[f"mm2t{tp_}"], tag=f"d2e{idx}")
                    p.op("vector", lambda e: e.tensor_tensor(
                        pk1sb[bpar][:, tt:tt + 1], mm2t[:], pk0sb[bpar][:, tt:tt + 1], ALU.mult),
                        reads=[f"mm2t{tp_}", f"pk0sb{bpar}"], writes=[f"pk1sb{bpar}"], tag=f"p2{idx}")
                    p.dma("scalar", lambda e, bb=bb, tt=tt: e.dma_start(
                        out=ef_h[bb, tt * 128:(tt + 1) * 128, :], in_=ipack[:]),
                        reads=[f"ipack{tp_}"], writes=["ef_h"], channel="pks", tag=f"se{idx}")


            def emit_routing(bb):
                p.dma("gpsimd", lambda e, bb=bb: e.dma_start(
                    out=efr[:], in_=ef_h[bb].rearrange("t k -> (t k)").unsqueeze(0)),
                    reads=["ef_h", "efr"], writes=["efr"], channel="rld", tag=f"lef{bb}")
                for c0 in range(0, S, 512):
                    cp_ = (c0 // 512) % 2
                    p.op("tensor", lambda e, c0=c0, cp_=cp_: e.matmul(
                        psSC[cp_][0:8, :], ones18[:], efr[:, c0:c0 + 512],
                        start=True, stop=True),
                        reads=["efr", "ones18"], writes=[f"psSC{cp_}"], tag=f"bce{bb}_{c0}")
                    p.op("vector", lambda e, c0=c0, cp_=cp_: e.tensor_scalar(
                        gA[:, c0:c0 + 512], psSC[cp_][0:8, :], iotaE8[:], None, ALU.is_equal),
                        reads=[f"psSC{cp_}", "iotaE8"], writes=["gA"], tag=f"oh{bb}_{c0}")
                p.op("vector", lambda e: e.tensor_tensor_scan(
                    gB[:], gA[:], onesS[:], 0.0, ALU.add, ALU.mult),
                    reads=["gA", "onesS"], writes=["gB"], tag=f"scan{bb}")
                p.op("vector", lambda e: e.scalar_tensor_tensor(
                    gC[:], gB[:], float(CAP), gA[:], ALU.is_le, ALU.mult),
                    reads=["gB", "gA"], writes=["gC"], tag=f"keep{bb}")
                p.op("vector", lambda e: e.tensor_scalar(
                    gB[:], gB[:], -1.0, float(CAP - 1), ALU.add, ALU.min),
                    reads=["gB"], writes=["gB"], tag=f"posc{bb}")
                p.op("vector", lambda e, bb=bb: e.tensor_scalar(
                    gB[:], gB[:], colER[:], float(bb * CAP - DUMP), ALU.add, ALU.add),
                    reads=["gB", "colER"], writes=["gB"], tag=f"poff{bb}")
                p.op("vector", lambda e: e.tensor_tensor(gC[:], gC[:], gB[:], ALU.mult),
                     reads=["gC", "gB"], writes=["gC"], tag=f"idxg{bb}")
                for c0 in range(0, S, 512):
                    cp_ = (c0 // 512) % 2
                    p.op("tensor", lambda e, c0=c0, cp_=cp_: e.matmul(
                        psSC[cp_][0:1, :], ones81[:], gC[:, c0:c0 + 512],
                        start=True, stop=True),
                        reads=["gC", "ones81"], writes=[f"psSC{cp_}"], tag=f"sid{bb}_{c0}")
                    p.op("vector", lambda e, c0=c0, cp_=cp_: e.tensor_scalar(
                        efr[:, c0:c0 + 512], psSC[cp_][0:1, :], float(DUMP), None, ALU.add),
                        reads=[f"psSC{cp_}"], writes=["efr"], tag=f"idq{bb}_{c0}")
                p.op("vector", lambda e: e.tensor_copy(idxi[:], efr[:]),
                     reads=["efr"], writes=["idxi"], tag=f"cvt{bb}")
                p.dma("gpsimd", lambda e, bb=bb: e.dma_start(out=idxq_h[bb, :], in_=idxi[0:1, :]),
                      reads=["idxi"], writes=["idxq_h"], channel="rst", tag=f"sxq{bb}")
                for k in range(2):
                    idxw = idxw_[k]
                    p.dma("gpsimd", lambda e, bb=bb, k=k, idxw=idxw: e.dma_start(
                        out=idxw[0:16, :],
                        in_=idxq_h[bb].rearrange("(c p k) -> k p c", p=16, k=2)[k],
                        allow_slow_non_contiguous=True),
                        reads=["idxq_h", f"idxw{k}"], writes=[f"idxw{k}"], channel=f"iw{k}",
                        tag=f"wr{bb}{k}")
                    for rep in range(1, 8):
                        p.dma("gpsimd", lambda e, rep=rep, idxw=idxw: e.dma_start(
                            out=idxw[16 * rep:16 * rep + 16, :], in_=idxw[0:16, :]),
                            reads=[f"idxw{k}"], writes=[f"idxw{k}"], channel=f"iwr{k}",
                            tag=f"wrr{bb}{k}{rep}")
                    p.dma("gpsimd", lambda e, bb=bb, k=k: e.dma_scatter_add(
                        out_ap=glist_h[:, 0:1],
                        in_ap=valw[:, bb * NT:(bb + 1) * NT].rearrange("p (a c) -> p c a", a=1),
                        idxs_ap=idxw_[k][:], num_idxs=T, num_idxs_reg=T, elem_size=1, elem_step=64),
                        reads=["valw", f"idxw{k}", "glist"], writes=["glist"],
                        channel="sctg", tag=f"gsc{bb}{k}")
                    pksrc = (pk0sb, pk1sb)[k][bb]
                    pkkey = f"pk{k}sb{bb}"
                    p.dma("gpsimd", lambda e, pksrc=pksrc, k=k: e.dma_scatter_add(
                        out_ap=gate_h[:, 0:1],
                        in_ap=pksrc[:].rearrange("p (c a) -> p c a", a=1),
                        idxs_ap=idxw_[k][:], num_idxs=T, num_idxs_reg=T, elem_size=1, elem_step=64),
                        reads=[pkkey, f"idxw{k}", "gate"], writes=["gate", pkkey],
                        channel="sctp", tag=f"psc{bb}{k}")
                    p.dma("gpsimd", lambda e, bb=bb, k=k: e.dma_start(
                        out=ydix[bb][k][0:16, :],
                        in_=idxq_h[bb].rearrange("(c p k) -> k p c", p=16, k=2)[k],
                        allow_slow_non_contiguous=True),
                        reads=["idxq_h", f"ydix{bb}_{k}"], writes=[f"ydix{bb}_{k}"],
                        channel=f"iw{k}", tag=f"ywA{bb}{k}")
                    for rep in range(1, 8):
                        p.dma("gpsimd", lambda e, bb=bb, k=k, rep=rep: e.dma_start(
                            out=ydix[bb][k][16 * rep:16 * rep + 16, :], in_=ydix[bb][k][0:16, :]),
                            reads=[f"ydix{bb}_{k}"], writes=[f"ydix{bb}_{k}"],
                            channel=f"iwr{k}", tag=f"ywAr{bb}{k}{rep}")

            GCHa = 640 if R % 640 == 0 else min(R, 512)
            NGCHa = R // GCHa

            def prep_lists(e_):
                pe = e_ % 2
                for gq in range(NGCHa):
                    r0 = gq * GCHa
                    p.dma("sync", lambda e, e_=e_, r0=r0: e.dma_start(
                        out=glf[:, 0:GCHa // 16],
                        in_=glist_h[e_ * R + r0:e_ * R + r0 + GCHa, 0:1].rearrange(
                            "(c p) o -> p (c o)", p=16),
                        allow_slow_non_contiguous=True),
                        reads=["glist", "glf"], writes=["glf"], channel=f"glr{e_ % 2}",
                        tag=f"glf{e_}_{gq}")
                    p.op("vector", lambda e, pe=pe, gq=gq: e.tensor_copy(
                        glx[pe][0:16, gq * (GCHa // 16):(gq + 1) * (GCHa // 16)],
                        glf[:, 0:GCHa // 16]),
                        reads=["glf"], writes=[f"glx{pe}"], tag=f"glc{e_}_{gq}")
                for rep in range(1, 8):
                    p.dma("scalar", lambda e, pe=pe, rep=rep: e.dma_start(
                        out=glx[pe][16 * rep:16 * rep + 16, :], in_=glx[pe][0:16, :]),
                        reads=[f"glx{pe}"], writes=[f"glx{pe}"], channel=f"glr2{pe}",
                        tag=f"glrp{e_}{rep}")
                p.dma("sync", lambda e, e_=e_, pe=pe: e.dma_start(
                    out=gatec[pe][:],
                    in_=gate_h[e_ * R:(e_ + 1) * R, 0:1].rearrange("(c p) o -> p (c o)", p=128),
                    allow_slow_non_contiguous=True),
                    reads=["gate", f"gatec{pe}"], writes=[f"gatec{pe}"], channel=f"glr{pe}",
                    tag=f"gld{e_}")

            for step, (bb, tt) in enumerate(ORDER):
                emit_tile_p1(bb, tt, step)
                if step > 0:
                    pb, ptt = ORDER[step - 1]
                    emit_tile_p2(pb, ptt, step - 1)
                if tt == 1 and bb > 0:
                    emit_routing(bb - 1)
            lb, ltt = ORDER[-1]
            emit_tile_p2(lb, ltt, len(ORDER) - 1)
            emit_routing(B - 1)
            prep_lists(0)

            p.barrier()

        ctxB = contextlib.ExitStack()
        with ctxB:
            sbb, psb = mk(ctxB)
            GCH = 640 if R % 640 == 0 else min(R, 512)
            NGCH = R // GCH
            SG = 320 if GCH % 320 == 0 else GCH
            assert R % GCH == 0 and GCH < 1024 and GCH % SG == 0 and SG <= 512
            xnTe = [sbb(f"xnTe{i}", [128, NGCH, FB, GCH], BF16) for i in range(2)]
            w1g = [sbb(f"w1g{i}", [128, FB, 128], BF16) for i in range(4)]
            w2sb = sbb("w2sb", [128, HB, C], BF16)
            hT = sbb("hT", [128, HB, R], BF16)
            ysb = [sbb(f"ysb{i}", [128, C], BF16) for i in range(2)]
            b2r = [sbb(f"b2r{i}", [1, C], BF16) for i in range(2)]
            psH = [psb(f"psH{i}", [128, 512]) for i in range(2)]
            psY = [psb(f"psY{i}", [128, C]) for i in range(2)]

            def prep_expert(e_):
                pe = e_ % 2
                if e_ > 0:
                    prep_lists(e_)
                for gq in range(NGCH):
                    r0 = gq * GCH
                    p.dma("gpsimd", lambda e, pe=pe, gq=gq: e.dma_gather(
                        out_ap=xnTe[pe][:, gq, :, :], in_ap=xn_rows,
                        idxs_ap=glx[pe][:, gq * (GCH // 16):(gq + 1) * (GCH // 16)],
                        num_idxs=GCH, num_idxs_reg=GCH, elem_size=C, transpose=True),
                        reads=["xn_rows", f"glx{pe}", f"xnTe{pe}"], writes=[f"xnTe{pe}"],
                        channel=f"gth{pe}", tag=f"gth{e_}_{gq}")

            def load_w2_chunk(e_, ck):
                hb0 = ck * (HB // 8)
                hb1 = (ck + 1) * (HB // 8)
                p.dma("gpsimd", lambda e, e_=e_, hb0=hb0, hb1=hb1: e.dma_start(
                    out=w2sb[:, hb0:hb1, :],
                    in_=w2_h[e_].rearrange("(hb p) c -> p hb c", p=128)[:, hb0:hb1, :]),
                    reads=["w2sb"], writes=["w2sb"], channel="w2l", tag=f"w2{e_}_{ck}")

            def load_w2(e_):
                load_w2_chunk(e_, 0)
                p.dma("gpsimd", lambda e, e_=e_: e.dma_start(
                    out=b2r[e_ % 2][:], in_=b2_h[e_].unsqueeze(0)),
                    reads=[f"b2r{e_ % 2}"], writes=[f"b2r{e_ % 2}"], channel="bl",
                    tag=f"b2{e_}")

            def load_w1g(e_, hb, buf):
                p.dma("gpsimd", lambda e, e_=e_, hb=hb, buf=buf: e.dma_start(
                    out=w1g[buf][:],
                    in_=w1_h[e_].rearrange("(fb p) h -> p fb h", p=128)[:, :, hb * 128:(hb + 1) * 128]),
                    reads=[f"w1g{buf}"], writes=[f"w1g{buf}"], channel=f"w1l{buf}",
                    tag=f"w1{e_}_{hb}")

            p.op("vector", lambda e: e.memset(ysb[0][:], 0.0), writes=["ysb0"], tag="zy")
            p.dma("scalar", lambda e: e.dma_start(out=y_store[ER:ER + 1, :], in_=ysb[0][0:1, :]),
                  reads=["ysb0"], writes=["y_store"], channel="ys0", tag="zyr")

            prep_expert(0)
            for j in range(3):
                if j < E * HB:
                    load_w1g(j // HB, j % HB, j % 4)
            for e_ in range(E):
                pe = e_ % 2
                load_w2(e_)
                if e_ + 1 < E:
                    prep_expert(e_ + 1)
                for hb in range(HB):
                    nh = e_ * HB + hb + 3
                    if nh < E * HB:
                        load_w1g(nh // HB, nh % HB, nh % 4)
                    step = max(1, HB // 8)
                    if (hb + 1) % step == 0 and (hb + 1) // step < 8:
                        load_w2_chunk(e_, (hb + 1) // step)
                    for rg in range(R // SG):
                        gq, so = divmod(rg * SG, GCH)
                        p_h = psH[rg % 2]
                        for fb in range(FB):
                            p.op("tensor", lambda e, hb=hb, rg=rg, fb=fb, pe=pe, gq=gq, so=so, p_h=p_h: e.matmul(
                                p_h[:, 0:SG], w1g[(e_ * HB + hb) % 4][:, fb, :],
                                xnTe[pe][:, gq, fb, so:so + SG],
                                start=(fb == 0), stop=(fb == FB - 1)),
                                reads=[f"w1g{(e_ * HB + hb) % 4}", f"xnTe{pe}"], writes=[f"psH{rg % 2}"],
                                tag=f"h{e_}_{hb}_{rg}_{fb}")
                        p.op("scalar", lambda e, hb=hb, rg=rg, e_=e_, p_h=p_h: e.activation(
                            hT[:, hb, rg * SG:(rg + 1) * SG], p_h[:, 0:SG], AF.Relu,
                            bias=b1all[:, e_ * HB + hb:e_ * HB + hb + 1], scale=1.0),
                            reads=[f"psH{rg % 2}", "b1all"], writes=["hT"],
                            tag=f"he{e_}_{hb}_{rg}")
                NH = C // 512
                for tc in range(RT):
                    for hf in range(NH):
                        for hb in range(HB):
                            p.op("tensor", lambda e, tc=tc, hb=hb, hf=hf: e.matmul(
                                psY[tc % 2][:, hf * 512:(hf + 1) * 512],
                                hT[:, hb, tc * 128:(tc + 1) * 128],
                                w2sb[:, hb, hf * 512:(hf + 1) * 512],
                                start=(hb == 0), stop=False, skip_group_check=True),
                                reads=["hT", "w2sb"], writes=[f"psY{tc % 2}"],
                                tag=f"ya{e_}_{tc}_{hf}_{hb}")
                        p.op("tensor", lambda e, tc=tc, pe=pe, hf=hf: e.matmul(
                            psY[tc % 2][:, hf * 512:(hf + 1) * 512], onesK1b[:],
                            b2r[pe][0:1, hf * 512:(hf + 1) * 512],
                            start=False, stop=True, skip_group_check=True),
                            reads=["onesK1b", f"b2r{pe}"], writes=[f"psY{tc % 2}"],
                            tag=f"yb{e_}_{tc}_{hf}")
                    p.op("scalar", lambda e, tc=tc, pe=pe: e.activation(
                        ysb[tc % 2][:], psY[tc % 2][:], AF.Identity,
                        scale=gatec[pe][:, tc:tc + 1]),
                        reads=[f"psY{tc % 2}", f"gatec{pe}"], writes=[f"ysb{tc % 2}"],
                        tag=f"ye{e_}_{tc}")
                    p.dma("scalar", lambda e, e_=e_, tc=tc: e.dma_start(
                        out=y_store[e_ * R + tc * 128: e_ * R + (tc + 1) * 128, :],
                        in_=ysb[tc % 2][:]),
                        reads=[f"ysb{tc % 2}"], writes=["y_store"], channel=f"ys{tc % 2}",
                        tag=f"yst{e_}_{tc}")

            p.barrier()

        ctxC = contextlib.ExitStack()
        with ctxC:
            sbc, psc = mk(ctxC)
            gth = [[sbc(f"gthc{k}_{i}", [128, NT * C], BF16) for i in range(2)]
                   for k in range(2)]
            xtc = [sbc(f"xtc{i}", [128, C], F32) for i in range(2)]
            stc = [sbc(f"stc{i}", [128, C], BF16) for i in range(2)]
            gpre = [sbc(f"gpre{i}", [128, C], F32) for i in range(2)]
            otc = [sbc(f"otc{i}", [128, C], F32) for i in range(2)]

            for bb in range(B):
                bp = bb % 2
                YCH = min(T, 1024)
                NYCH = T // YCH
                for k in range(2):
                    for yq in range(NYCH):
                        p.dma("gpsimd", lambda e, bb=bb, k=k, bp=bp, yq=yq: e.dma_gather(
                            out_ap=gth[k][bp][:].rearrange("p (n c) -> p n c", n=NT)[
                                :, yq * (YCH // 128):(yq + 1) * (YCH // 128), :],
                            in_ap=y_store,
                            idxs_ap=ydix[bb][k][:, yq * (YCH // 16):(yq + 1) * (YCH // 16)],
                            num_idxs=YCH, num_idxs_reg=YCH, elem_size=C, transpose=False),
                            reads=["y_store", f"ydix{bb}_{k}", f"gthc{k}_{bp}"],
                            writes=[f"gthc{k}_{bp}"], channel=f"gbk{k}", tag=f"gb{bb}{k}_{yq}")
                for tc in range(NT):
                    pc = tc % 2
                    p.dma("sync", lambda e, bb=bb, tc=tc, pc=pc: e.dma_start(
                        out=xtc[pc][:], in_=x_in[bb, tc * 128:(tc + 1) * 128, :]),
                        reads=[f"xtc{pc}"], writes=[f"xtc{pc}"], channel=f"xc{pc}",
                        tag=f"xc{bb}_{tc}")
                    p.dma("sync", lambda e, bb=bb, tc=tc, pc=pc: e.dma_start(
                        out=stc[pc][:], in_=s_rows[bb * T + tc * 128: bb * T + (tc + 1) * 128, :]),
                        reads=["s_rows", f"stc{pc}"], writes=[f"stc{pc}"], channel=f"scl{pc}",
                        tag=f"scl{bb}_{tc}")
                    p.op("vector", lambda e, tc=tc, pc=pc, bp=bp: e.tensor_tensor(
                        gpre[pc][:], gth[0][bp][:, tc * C:(tc + 1) * C],
                        gth[1][bp][:, tc * C:(tc + 1) * C], ALU.add),
                        reads=[f"gthc0_{bp}", f"gthc1_{bp}"], writes=[f"gpre{pc}"],
                        tag=f"gaa{bb}_{tc}")
                    p.op("scalar", lambda e, pc=pc: e.activation(gpre[pc][:], gpre[pc][:], AF.Silu),
                         reads=[f"gpre{pc}"], writes=[f"gpre{pc}"], tag=f"si{bb}_{tc}")
                    p.op("vector", lambda e, pc=pc: e.tensor_tensor(
                        gpre[pc][:], stc[pc][:], gpre[pc][:], ALU.mult),
                        reads=[f"stc{pc}", f"gpre{pc}"], writes=[f"gpre{pc}"], tag=f"sg{bb}_{tc}")
                    p.op("vector", lambda e, pc=pc: e.tensor_tensor(
                        otc[pc][:], xtc[pc][:], gpre[pc][:], ALU.add),
                        reads=[f"xtc{pc}", f"gpre{pc}"], writes=[f"otc{pc}"], tag=f"ob{bb}_{tc}")
                    p.dma("sync", lambda e, bb=bb, tc=tc, pc=pc: e.dma_start(
                        out=out_h[bb, tc * 128:(tc + 1) * 128, :], in_=otc[pc][:]),
                        reads=[f"otc{pc}"], writes=["outw"], channel=f"os{pc}",
                        tag=f"os{bb}_{tc}")

            p.op("sync", lambda e: e.nop(), reads=["outw"], tag="fin")

        with nc.Block() as blk:
            p.build(blk, ctx)
    return nc


def kernel(L0_attn_residual, L0_attn_w_fc, L0_attn_w_proj, L0_attn_ln_g,
           L0_attn_ln_b, L0_ffn_Wr, L0_ffn_W1, L0_ffn_b1, L0_ffn_W2, L0_ffn_b2):
    cfg = default_cfg()
    n_cores = 8
    B = cfg["B"]
    x = np.ascontiguousarray(np.asarray(L0_attn_residual, dtype=np.float32))
    shared = {
        "w_fc": np.ascontiguousarray(np.asarray(L0_attn_w_fc, dtype=np.float32)),
        "w_proj": np.ascontiguousarray(np.asarray(L0_attn_w_proj, dtype=np.float32)),
        "ln_g": np.ascontiguousarray(np.asarray(L0_attn_ln_g, dtype=np.float32)),
        "ln_b": np.ascontiguousarray(np.asarray(L0_attn_ln_b, dtype=np.float32)),
        "wr": np.ascontiguousarray(np.asarray(L0_ffn_Wr, dtype=np.float32)),
        "w1": np.ascontiguousarray(np.asarray(L0_ffn_W1, dtype=np.float32)),
        "b1": np.ascontiguousarray(np.asarray(L0_ffn_b1, dtype=np.float32)),
        "w2": np.ascontiguousarray(np.asarray(L0_ffn_W2, dtype=np.float32)),
        "b2": np.ascontiguousarray(np.asarray(L0_ffn_b2, dtype=np.float32)),
    }
    nc = build_nc(cfg)
    if not nc.is_finalized():
        nc.finalize()
    in_maps = []
    for c in range(n_cores):
        m = dict(shared)
        m["x"] = np.ascontiguousarray(x[c * B:(c + 1) * B])
        in_maps.append(m)
    res = run_bass_kernel_spmd(nc, in_maps, core_ids=list(range(n_cores)))
    out = np.concatenate([np.asarray(r["out"]) for r in res.results], axis=0)
    return out.astype(np.float32)
